# version 21
# baseline (speedup 1.0000x reference)
import numpy as np
import ml_dtypes
import copy
from contextlib import ExitStack

import concourse.bass as bass
import concourse.mybir as mybir
from concourse.bass_utils import run_bass_kernel_spmd

F32 = mybir.dt.float32
BF16 = mybir.dt.bfloat16
AF = mybir.ActivationFunctionType
ALU = mybir.AluOpType
AX = mybir.AxisListType

MASK_NEG = -30000.0
EPS = 1e-5

CFG_FULL = dict(S=4096, D=2048, QL=1536, KVL=512, H=16, F=8192)
CFG_SMALL = dict(S=1024, D=512, QL=256, KVL=256, H=2, F=1024)

ROPE = 64
NOPE = 64
VD = 128
HD = NOPE + ROPE
HKV = NOPE + VD
SEG = 2048

ENGINES = ("pe", "dve", "act", "gp", "sp")


def seg_list(E):
    segs = []
    rem = E
    while rem > SEG:
        segs.append(SEG)
        rem -= SEG
    segs.append(rem)
    return segs


def slab_list(width):
    out = []
    k = 0
    while k < width:
        w = min(512, width - k)
        out.append((k, w))
        k += w
    return out


def swap_view(ap_2d, n):
    return bass.AP(ap_2d.tensor, ap_2d.offset + 1,
                   [list(ap_2d.ap[0]), [2, n // 2], [-1, 2]])


class Ctx:

    def __init__(self, block, handles):
        self.block = block
        self.handles = handles
        self.state = {"sems": {n: 0 for n in handles},
                      "wm": {e: {} for e in ENGINES}}
        self.cur = None
        self.eng = None

    def snap(self):
        return copy.deepcopy(self.state)

    def restore(self, s):
        self.state = copy.deepcopy(s)

    def wait(self, engine, sem, val):
        if val <= 0:
            return
        wm = self.state["wm"][engine]
        if wm.get(sem, 0) >= val:
            return
        wm[sem] = val
        if self.cur == engine:
            self.eng.wait_ge(self.handles[sem], val)

    def waitt(self, engine, tok):
        if tok is not None:
            self.wait(engine, tok[0], tok[1])

    def ins(self, engine, fn, inc=None, k=1):
        if inc is not None:
            self.state["sems"][inc] += k
        if self.cur == engine:
            bi = fn(self.eng)
            if inc is not None:
                bi.then_inc(self.handles[inc], k)
        return (inc, self.state["sems"][inc]) if inc is not None else None

    def tok(self, sem):
        return (sem, self.state["sems"][sem])

    def run_phase(self, phase_fn):
        s0 = self.snap()
        calls = [("pe", self.block.tensor), ("dve", self.block.vector),
                 ("act", self.block.scalar), ("gp", self.block.gpsimd),
                 ("sp", self.block.sync)]
        for name, bc in calls:
            self.restore(s0)

            def body(eng, name=name):
                self.cur = name
                self.eng = eng
                phase_fn(self)
                self.cur = None
                self.eng = None

            bc(body)

    def full_gate(self):
        def g(c):
            for e in ENGINES:
                for sem, v in list(c.state["sems"].items()):
                    c.wait(e, sem, v)
        self.run_phase(g)


def build_program(cfg):
    S, D, QL, KVL, H, F = cfg["S"], cfg["D"], cfg["QL"], cfg["KVL"], cfg["H"], cfg["F"]
    KVOUT = KVL + ROPE
    DC, QLC, KVC, FC = D // 128, QL // 128, KVL // 128, F // 128
    NT, NO = S // 128, S // 256
    TOK = NO * 128
    TH = TOK // 2
    NTH = TH // 128
    SCALE = float(HD) ** -0.5
    SUP = 512
    NSUP = F // SUP
    SUPC = SUP // 128

    nc = bass.Bass("TRN2", target_bir_lowering=False)

    x_d = nc.dram_tensor("x", [S, D], F32, kind="ExternalInput")
    wqa_d = nc.dram_tensor("wqa", [D, QL], F32, kind="ExternalInput")
    wqb_d = nc.dram_tensor("wqb", [QL, H * HD], F32, kind="ExternalInput")
    wkva_d = nc.dram_tensor("wkva", [D, KVOUT], F32, kind="ExternalInput")
    wkvb_d = nc.dram_tensor("wkvb", [KVL, H * HKV], F32, kind="ExternalInput")
    wo_d = nc.dram_tensor("wo", [H * VD, D], F32, kind="ExternalInput")
    wgT_d = nc.dram_tensor("wgT", [D, F], F32, kind="ExternalInput")
    wuT_d = nc.dram_tensor("wuT", [D, F], F32, kind="ExternalInput")
    wdT_d = nc.dram_tensor("wdT", [F, D], F32, kind="ExternalInput")
    idnf_d = nc.dram_tensor("idnf", [128, 128], F32, kind="ExternalInput")
    idnb_d = nc.dram_tensor("idnb", [128, 128], BF16, kind="ExternalInput")
    cosq_d = nc.dram_tensor("cosq", [128, NO * ROPE], F32, kind="ExternalInput")
    sinq_d = nc.dram_tensor("sinq", [128, NO * ROPE], F32, kind="ExternalInput")
    cosk_d = nc.dram_tensor("cosk", [128, NT * ROPE], F32, kind="ExternalInput")
    sink_d = nc.dram_tensor("sink", [128, NT * ROPE], F32, kind="ExternalInput")
    masks_d = nc.dram_tensor("masks", [128, NO * 512], BF16, kind="ExternalInput")
    y_d = nc.dram_tensor("y", [TOK, D], F32, kind="ExternalOutput")
    qlat_sc = nc.dram_tensor("qlat_sc", [NO, 128, QL], BF16, kind="Internal")
    q_sc = nc.dram_tensor("q_sc", [NO, 128, H * HD], BF16, kind="Internal")
    attn_sc = nc.dram_tensor("attn_sc", [H, 128, TOK], BF16, kind="Internal")
    kmsc = nc.dram_tensor("kmsc", [2, 128], BF16, kind="Internal")

    xa = x_d.ap().rearrange("(t r) d -> t r d", r=128)
    ya = y_d.ap().rearrange("(t r) d -> t r d", r=128)
    qlat_a = qlat_sc.ap()
    q_a = q_sc.ap()
    attn_a = attn_sc.ap()

    with ExitStack() as st:
        idn_f = st.enter_context(nc.sbuf_tensor("idn_f", [128, 128], F32))
        idn_b = st.enter_context(nc.sbuf_tensor("idn_b", [128, 128], BF16))
        ones_neg = st.enter_context(nc.sbuf_tensor("ones_neg", [128, 1], BF16))
        ones_inv = st.enter_context(nc.sbuf_tensor("ones_inv", [128, 1], BF16))
        ones_row = st.enter_context(nc.sbuf_tensor("ones_row", [1, 128], F32))
        sm = st.enter_context(nc.sbuf_tensor("sm", [128, 32], F32))
        bnb = st.enter_context(nc.sbuf_tensor("bnb", [128, 24], F32))
        oacc = st.enter_context(nc.sbuf_tensor("oacc", [128, 128], F32))
        epsb = st.enter_context(nc.sbuf_tensor("epsb", [128, 1], F32))

        pex = st.enter_context(nc.psum_tensor("pex", [128, 512], F32))
        pvb = st.enter_context(nc.psum_tensor("pvb", [128, 512], F32))
        ptrA = st.enter_context(nc.psum_tensor("ptrA", [128, 1024], BF16))

        semnames = ["dS", "dG", "dW0", "dW1", "dW2", "dW3", "dX0", "dX1", "dQ0", "dQ1", "dL0", "dL1", "dY0", "dY1", "dA0", "dA1", "tP", "tV", "tA", "tG"]
        handles = {n: st.enter_context(nc.semaphore(n)) for n in semnames}
        block = st.enter_context(nc.Block())
        c = Ctx(block, handles)


        def ph_pro(cc):
            cc.ins("sp", lambda e: e.dma_start(out=idn_f[:], in_=idnf_d.ap()), inc="dS", k=16)
            cc.ins("sp", lambda e: e.dma_start(out=idn_b[:], in_=idnb_d.ap()), inc="dS", k=16)
            cc.ins("dve", lambda e: e.memset(ones_neg[:], -1.0 / D), inc="tV")
            cc.ins("dve", lambda e: e.memset(ones_inv[:], 1.0 / D), inc="tV")
            cc.ins("dve", lambda e: e.memset(ones_row[:], 1.0), inc="tV")
            cc.ins("dve", lambda e: e.memset(epsb[:], EPS), inc="tV")

        c.run_phase(ph_pro)
        c.full_gate()

        with ExitStack() as stAB:
            c_kv = stAB.enter_context(nc.sbuf_tensor("c_kv", [128, KVC * S], BF16))
            kpe_bf = stAB.enter_context(nc.sbuf_tensor("kpe_bf", [64, S], BF16))
            wkvb_bf = stAB.enter_context(nc.sbuf_tensor("wkvb_bf", [128, KVC * H * HKV], BF16))

            with ExitStack() as stA:
                ps4 = stA.enter_context(nc.psum_tensor("psA1", [128, 1024], F32))
                ptrB = stA.enter_context(nc.psum_tensor("ptrB1", [128, 1024], BF16))
                cosk = stA.enter_context(nc.sbuf_tensor("cosk_s", [128, NT * ROPE], F32))
                sink = stA.enter_context(nc.sbuf_tensor("sink_s", [128, NT * ROPE], F32))
                xring = stA.enter_context(nc.sbuf_tensor("xring", [128, 2 * D], F32))
                wqa_bf = stA.enter_context(nc.sbuf_tensor("wqa_bf", [128, DC * QL], BF16))
                wkva_bf = stA.enter_context(nc.sbuf_tensor("wkva_bf", [128, DC * KVOUT], BF16))
                wst = stA.enter_context(nc.sbuf_tensor("wst", [128, 2 * 2048], F32))
                xn1tm = stA.enter_context(nc.sbuf_tensor("xn1tm", [128, 2 * D], BF16))
                xn1fm = stA.enter_context(nc.sbuf_tensor("xn1fm", [128, 2 * D], BF16))
                kst = stA.enter_context(nc.sbuf_tensor("kst", [128, 192], F32))
                qlst = stA.enter_context(nc.sbuf_tensor("qlst", [128, 2 * QL], BF16))

                def ph_a1(cc):
                    pieces = []
                    for dcx in range(DC):
                        pieces.append((wkva_d.ap(), dcx, 0, KVOUT, wkva_bf, dcx * KVOUT))
                    for dcx in range(DC):
                        off = 0
                        while off < QL:
                            w = min(2048, QL - off)
                            pieces.append((wqa_d.ap(), dcx, off, w, wqa_bf, dcx * QL + off))
                            off += w
                    for ccx in range(KVC):
                        off = 0
                        while off < H * HKV:
                            w = min(2048, H * HKV - off)
                            pieces.append((wkvb_d.ap(), ccx, off, w, wkvb_bf, ccx * H * HKV + off))
                            off += w
                    npc = len(pieces)
                    cast_tok = [None] * npc
                    kva_last = [None]
                    qa_last = [None]

                    for j in range(npc):
                        src, chk, off, w, dst, doff = pieces[j]
                        slot = j % 2
                        if j >= 2:
                            cc.waitt("sp", cast_tok[j - 2])
                        dw = "dW0" if slot == 0 else "dW1"
                        cc.ins("sp", lambda e, src=src, chk=chk, off=off, w=w, slot=slot:
                               e.dma_start(out=wst[:, slot * 2048: slot * 2048 + w],
                                           in_=src[chk * 128:(chk + 1) * 128, off:off + w]),
                               inc=dw, k=16)
                        cc.wait("gp", dw, cc.state["sems"][dw])
                        tokc = cc.ins("gp", lambda e, slot=slot, w=w, dst=dst, doff=doff:
                                      e.tensor_copy(out=dst[:, doff:doff + w],
                                                    in_=wst[:, slot * 2048: slot * 2048 + w]),
                                      inc="tG")
                        cast_tok[j] = tokc
                        if j == DC - 1:
                            kva_last[0] = tokc
                        if j == npc - KVC * ((H * HKV + 2047) // 2048) - 1:
                            qa_last[0] = tokc

                    for buf, src in ((cosk, cosk_d), (sink, sink_d)):
                        cc.ins("sp", lambda e, buf=buf, src=src:
                               e.dma_start(out=buf[:], in_=src.ap()), inc="dS", k=16)
                    tabs_tok = cc.tok("dS")

                    xr_free = [None, None]
                    tm_free = [None, None]
                    fm_free = [None, None]
                    ql_free = [None, None]
                    ptr_free = [None, None]
                    bank_free = [None, None]
                    pex_free = [None]

                    for t in range(NT):
                        sl = t % 2
                        dx = "dX0" if sl == 0 else "dX1"
                        cc.waitt("gp", xr_free[sl])
                        cc.ins("gp", lambda e, t=t, sl=sl:
                               e.dma_start(out=xring[:, sl * D:(sl + 1) * D], in_=xa[t]),
                               inc=dx, k=16)
                        cc.wait("dve", dx, cc.state["sems"][dx])
                        nch = (D + 511) // 512
                        for j in range(nch):
                            w = min(512, D - j * 512)
                            cc.ins("dve", lambda e, sl=sl, j=j, w=w:
                                   e.bn_stats(bnb[:, j * 6:(j + 1) * 6],
                                              xring[:, sl * D + j * 512: sl * D + j * 512 + w]),
                                   inc="tV" if j == nch - 1 else None)
                        cc.wait("dve", "tV", cc.state["sems"]["tV"])
                        cc.ins("dve", lambda e, nch=nch: e.bn_aggr(sm[:, 0:2], bnb[:, 0:6 * nch]), inc="tV")
                        agg = cc.tok("tV")
                        cc.waitt("act", agg)
                        cc.ins("act", lambda e: e.activation(out=sm[:, 2:3], in_=sm[:, 1:2],
                                                             func=AF.Ln, bias=epsb[:, 0:1]), inc="tA")
                        cc.wait("act", "tA", cc.state["sems"]["tA"])
                        cc.ins("act", lambda e: e.activation(out=sm[:, 2:3], in_=sm[:, 2:3],
                                                             func=AF.Exp, scale=-0.5), inc="tA")
                        rstd = cc.tok("tA")
                        cc.waitt("dve", rstd)
                        cc.waitt("dve", tm_free[sl])
                        cc.ins("dve", lambda e, sl=sl:
                               e.tensor_scalar(out=xn1tm[:, sl * D:(sl + 1) * D],
                                               in0=xring[:, sl * D:(sl + 1) * D],
                                               scalar1=sm[:, 0:1], scalar2=sm[:, 2:3],
                                               op0=ALU.subtract, op1=ALU.mult),
                               inc="tV")
                        norm = cc.tok("tV")
                        xr_free[sl] = norm
                        cc.waitt("pe", norm)
                        cc.waitt("pe", fm_free[sl])
                        for r in range(DC // 4):
                            half = r % 2
                            ptr_t = (ptrA, ptrB)[half]
                            cc.waitt("pe", ptr_free[half])
                            for j in range(4):
                                dcx = r * 4 + j
                                cc.ins("pe", lambda e, sl=sl, dcx=dcx, j=j, ptr_t=ptr_t:
                                       e.transpose(ptr_t[:, j * 128:(j + 1) * 128],
                                                   xn1tm[:, sl * D + dcx * 128: sl * D + (dcx + 1) * 128],
                                                   idn_b[:]),
                                       inc="tP" if j == 3 else None)
                            tr = cc.tok("tP")
                            cc.waitt("dve", tr)
                            cc.ins("dve", lambda e, sl=sl, r=r, ptr_t=ptr_t:
                                   e.tensor_copy(out=xn1fm[:, sl * D + r * 512: sl * D + (r + 1) * 512],
                                                 in_=ptr_t[:, 0:512]),
                                   inc="tV")
                            ptr_free[half] = cc.tok("tV")
                        tm_free[sl] = cc.tok("tP")
                        fm_ready = cc.tok("tV")

                        cc.waitt("pe", fm_ready)
                        cc.waitt("pe", kva_last[0])
                        oc_list = [("kva", o) for o in range(KVC + 1)]
                        if t % 2 == 0:
                            oc_list += [("qa", o) for o in range(QLC)]
                        for idx, (kind, oc) in enumerate(oc_list):
                            if kind == "qa":
                                cc.waitt("pe", qa_last[0])
                            bk = idx % 2
                            bko = bk * 512
                            cc.waitt("pe", bank_free[bk])
                            is_kpe = (kind == "kva" and oc == KVC)
                            m = 64 if is_kpe else 128
                            for dcx in range(DC):
                                if kind == "kva":
                                    lo = dcx * KVOUT + oc * 128
                                    lhs = wkva_bf
                                else:
                                    lo = dcx * QL + oc * 128
                                    lhs = wqa_bf
                                cc.ins("pe", lambda e, bko=bko, lhs=lhs, lo=lo, m=m, sl=sl, dcx=dcx,
                                       first=(dcx == 0), last=(dcx == DC - 1):
                                       e.matmul(ps4[0:m, bko:bko + 128],
                                                lhs[:, lo:lo + m],
                                                xn1fm[:, sl * D + dcx * 128: sl * D + (dcx + 1) * 128],
                                                start=first, stop=last),
                                       inc="tP" if dcx == DC - 1 else None)
                            grp = cc.tok("tP")
                            if kind == "kva" and oc < KVC:
                                cc.waitt("act", grp)
                                cc.ins("act", lambda e, bko=bko, oc=oc, t=t:
                                       e.activation(out=c_kv[:, oc * S + t * 128: oc * S + (t + 1) * 128],
                                                    in_=ps4[:, bko:bko + 128], func=AF.Copy),
                                       inc="tA")
                                bank_free[bk] = cc.tok("tA")
                            elif is_kpe:
                                cc.waitt("act", grp)
                                cc.ins("act", lambda e, bko=bko:
                                       e.activation(out=kst[0:64, 0:128],
                                                    in_=ps4[0:64, bko:bko + 128], func=AF.Copy),
                                       inc="tA")
                                kc = cc.tok("tA")
                                bank_free[bk] = kc
                                cc.waitt("pe", kc)
                                cc.waitt("pe", pex_free[0])
                                cc.ins("pe", lambda e: e.transpose(pex[:, 0:64], kst[0:64, 0:128], idn_f[0:64, 0:64]),
                                       inc="tP")
                                ktr = cc.tok("tP")
                                cc.waitt("dve", ktr)
                                cc.waitt("dve", tabs_tok)
                                cc.ins("dve", lambda e, t=t:
                                       e.tensor_tensor(out=kst[:, 64:128], in0=pex[:, 0:64],
                                                       in1=cosk[:, t * ROPE:(t + 1) * ROPE], op=ALU.mult),
                                       inc="tV")
                                cc.ins("dve", lambda e, t=t:
                                       e.tensor_tensor(out=kst[:, 128:192],
                                                       in0=swap_view(pex[:, 0:64], 64),
                                                       in1=sink[:, t * ROPE:(t + 1) * ROPE], op=ALU.mult),
                                       inc="tV")
                                cc.wait("dve", "tV", cc.state["sems"]["tV"])
                                cc.ins("dve", lambda e:
                                       e.tensor_tensor(out=kst[:, 64:128], in0=kst[:, 64:128],
                                                       in1=kst[:, 128:192], op=ALU.add),
                                       inc="tV")
                                krp = cc.tok("tV")
                                cc.waitt("pe", krp)
                                cc.ins("pe", lambda e: e.transpose(pex[0:64, 128:256], kst[:, 64:128], idn_f[:]),
                                       inc="tP")
                                ktr2 = cc.tok("tP")
                                cc.waitt("act", ktr2)
                                cc.ins("act", lambda e, t=t:
                                       e.activation(out=kpe_bf[0:64, t * 128:(t + 1) * 128],
                                                    in_=pex[0:64, 128:256], func=AF.Copy),
                                       inc="tA")
                                pex_free[0] = cc.tok("tA")
                            else:
                                qsl = (t // 2) % 2
                                cc.waitt("dve", grp)
                                if oc == 0:
                                    cc.waitt("dve", ql_free[qsl])
                                cc.ins("dve", lambda e, bko=bko, oc=oc, qsl=qsl:
                                       e.tensor_copy(out=qlst[:, qsl * QL + oc * 128: qsl * QL + (oc + 1) * 128],
                                                     in_=ps4[:, bko:bko + 128]),
                                       inc="tV")
                                bank_free[bk] = cc.tok("tV")
                                if oc == QLC - 1:
                                    qev = cc.tok("tV")
                                    cc.waitt("sp", qev)
                                    dl = "dL0" if qsl == 0 else "dL1"
                                    cc.ins("sp", lambda e, t=t, qsl=qsl:
                                           e.dma_start(out=qlat_a[t // 2],
                                                       in_=qlst[:, qsl * QL:(qsl + 1) * QL]),
                                           inc=dl, k=16)
                                    ql_free[qsl] = cc.tok(dl)
                        fm_free[sl] = cc.tok("tP")

                c.run_phase(ph_a1)
            c.full_gate()

            with ExitStack() as stA2:
                ps4 = stA2.enter_context(nc.psum_tensor("psA2", [128, 2048], F32))
                cosq = stA2.enter_context(nc.sbuf_tensor("cosq2", [128, NO * ROPE], F32))
                sinq = stA2.enter_context(nc.sbuf_tensor("sinq2", [128, NO * ROPE], F32))
                wqb_bf = stA2.enter_context(nc.sbuf_tensor("wqb_bf", [128, QLC * H * HD], BF16))
                wst2 = stA2.enter_context(nc.sbuf_tensor("wst2", [128, 2 * 2048], F32))
                qlring = stA2.enter_context(nc.sbuf_tensor("qlring", [128, 2 * QL], BF16))
                qstage = stA2.enter_context(nc.sbuf_tensor("qstage", [128, 2 * H * HD], BF16))

                def ph_a2(cc):
                    for buf, src in ((cosq, cosq_d), (sinq, sinq_d)):
                        cc.ins("sp", lambda e, buf=buf, src=src:
                               e.dma_start(out=buf[:], in_=src.ap()), inc="dS", k=16)
                    tabs = cc.tok("dS")
                    pieces = []
                    for lcx in range(QLC):
                        off = 0
                        while off < H * HD:
                            w = min(2048, H * HD - off)
                            pieces.append((lcx, off, w))
                            off += w
                    cast_tok = [None] * len(pieces)
                    wqb_last = [None]
                    pi = [0]

                    def emit_pieces(n):
                        j = pi[0]
                        for _ in range(n):
                            if j >= len(pieces):
                                break
                            lcx, off, w = pieces[j]
                            slot = j % 2
                            if j >= 2:
                                cc.waitt("sp", cast_tok[j - 2])
                            dw = "dW0" if slot == 0 else "dW1"
                            cc.ins("sp", lambda e, lcx=lcx, off=off, w=w, slot=slot:
                                   e.dma_start(out=wst2[:, slot * 2048: slot * 2048 + w],
                                               in_=wqb_d.ap()[lcx * 128:(lcx + 1) * 128, off:off + w]),
                                   inc=dw, k=16)
                            cc.wait("gp", dw, cc.state["sems"][dw])
                            tokc = cc.ins("gp", lambda e, slot=slot, w=w, lcx=lcx, off=off:
                                          e.tensor_copy(out=wqb_bf[:, lcx * H * HD + off: lcx * H * HD + off + w],
                                                        in_=wst2[:, slot * 2048: slot * 2048 + w]),
                                          inc="tG")
                            cast_tok[j] = tokc
                            if j == len(pieces) - 1:
                                wqb_last[0] = tokc
                            j += 1
                        pi[0] = j

                    emit_pieces(len(pieces))

                    qr_free = [None, None]
                    qs_free = [None, None]
                    nslab = (H * HD + 511) // 512
                    bank_free = [None] * 4

                    for i in range(NO):
                        sl = i % 2
                        dq = "dQ0" if sl == 0 else "dQ1"
                        cc.waitt("gp", qr_free[sl])
                        cc.ins("gp", lambda e, i=i, sl=sl:
                               e.dma_start(out=qlring[:, sl * QL:(sl + 1) * QL], in_=qlat_a[i]),
                               inc=dq, k=16)
                        qload = cc.tok(dq)
                        cc.waitt("pe", qload)
                        cc.waitt("pe", wqb_last[0])
                        if i == 0:
                            pass
                        for s in range(nslab):
                            w = min(512, H * HD - s * 512)
                            bk = s % 4
                            cc.waitt("pe", bank_free[bk])
                            for lcx in range(QLC):
                                cc.ins("pe", lambda e, bk=bk, w=w, lcx=lcx, s=s, sl=sl,
                                       first=(lcx == 0), last=(lcx == QLC - 1):
                                       e.matmul(ps4[:, bk * 512: bk * 512 + w],
                                                qlring[:, sl * QL + lcx * 128: sl * QL + (lcx + 1) * 128],
                                                wqb_bf[:, lcx * H * HD + s * 512: lcx * H * HD + s * 512 + w],
                                                start=first, stop=last),
                                       inc="tP" if lcx == QLC - 1 else None)
                            grp = cc.tok("tP")
                            cc.waitt("act", grp)
                            if s == 0:
                                cc.waitt("act", qs_free[sl])
                            cc.ins("act", lambda e, bk=bk, w=w, s=s, sl=sl:
                                   e.activation(out=qstage[:, sl * H * HD + s * 512: sl * H * HD + s * 512 + w],
                                                in_=ps4[:, bk * 512: bk * 512 + w], func=AF.Copy),
                                   inc="tA")
                            ev = cc.tok("tA")
                            cc.waitt("dve", grp)
                            cc.waitt("dve", ev)
                            cc.wait("dve", "dS", tabs[1])
                            for hh in range(w // HD):
                                pecol = bk * 512 + hh * HD + NOPE
                                qcol = sl * H * HD + s * 512 + hh * HD + NOPE
                                cc.wait("dve", "tV", cc.state["sems"]["tV"])
                                cc.ins("dve", lambda e, pecol=pecol, i=i:
                                       e.tensor_tensor(out=kst_a2[:, 0:64],
                                                       in0=ps4[:, pecol:pecol + 64],
                                                       in1=cosq[:, i * ROPE:(i + 1) * ROPE], op=ALU.mult),
                                       inc="tV")
                                cc.ins("dve", lambda e, pecol=pecol, i=i:
                                       e.tensor_tensor(out=kst_a2[:, 64:128],
                                                       in0=swap_view(ps4[:, pecol:pecol + 64], 64),
                                                       in1=sinq[:, i * ROPE:(i + 1) * ROPE], op=ALU.mult),
                                       inc="tV")
                                cc.wait("dve", "tV", cc.state["sems"]["tV"])
                                cc.ins("dve", lambda e, qcol=qcol:
                                       e.tensor_tensor(out=qstage[:, qcol:qcol + 64],
                                                       in0=kst_a2[:, 0:64], in1=kst_a2[:, 64:128],
                                                       op=ALU.add),
                                       inc="tV")
                            bank_free[bk] = cc.tok("tV")
                        qr_free[sl] = cc.tok("tP")
                        done = cc.tok("tV")
                        cc.waitt("sp", done)
                        cc.wait("sp", "tA", cc.state["sems"]["tA"])
                        dl = "dL0" if sl == 0 else "dL1"
                        cc.ins("sp", lambda e, i=i, sl=sl:
                               e.dma_start(out=q_a[i], in_=qstage[:, sl * H * HD:(sl + 1) * H * HD]),
                               inc=dl, k=16)
                        qs_free[sl] = cc.tok(dl)

                kst_a2 = stA2.enter_context(nc.sbuf_tensor("kst_a2", [128, 128], F32))
                c.run_phase(ph_a2)
            c.full_gate()

            with ExitStack() as stB:
                psB = stB.enter_context(nc.psum_tensor("psB", [128, 1536], F32))
                stgB = stB.enter_context(nc.psum_tensor("stgB", [128, 1024], BF16))
                pacc2 = stB.enter_context(nc.psum_tensor("pacc2", [128, 512], F32))
                masks_sb = stB.enter_context(nc.sbuf_tensor("masks_sb", [128, NO * 512], BF16))
                kfull = stB.enter_context(nc.sbuf_tensor("kfull", [128, 2 * S], BF16))
                v_tm = stB.enter_context(nc.sbuf_tensor("v_tm", [128, 2 * S], BF16))
                qh = stB.enter_context(nc.sbuf_tensor("qh", [128, 2 * NO * 128], BF16))
                qTb = stB.enter_context(nc.sbuf_tensor("qTb", [128, 2 * 128], BF16))
                qsq_sb = stB.enter_context(nc.sbuf_tensor("qsq_sb", [128, 128], BF16))
                kamb = stB.enter_context(nc.sbuf_tensor("kamb", [128, 18], BF16))
                kmbc = stB.enter_context(nc.sbuf_tensor("kmbc", [128, 2 * 128], BF16))
                ksq_sb = stB.enter_context(nc.sbuf_tensor("ksq_sb", [128, 512], BF16))
                kb2 = stB.enter_context(nc.sbuf_tensor("kb2", [128, 2], F32))
                bnd = stB.enter_context(nc.sbuf_tensor("bnd", [128, 2 * NO], F32))
                Pbuf = stB.enter_context(nc.sbuf_tensor("Pbuf", [128, 3 * 1024], BF16))
                PTs = stB.enter_context(nc.sbuf_tensor("PTs", [128, 2 * 1024], BF16))
                obf = stB.enter_context(nc.sbuf_tensor("obf", [128, 4 * 128], BF16))
                astage = stB.enter_context(nc.sbuf_tensor("astage", [128, 2 * TOK], BF16))

                NKS = S // 512
                ACCB = (pvb, pacc2)
                STG = (ptrA, stgB)
                PSOFF = (0, 1024)
                PSCAP = (1024, 512)

                def ph_b(cc):
                    cc.ins("sp", lambda e: e.dma_start(out=masks_sb[:], in_=masks_d.ap()), inc="dS", k=16)
                    masks_tok = cc.tok("dS")
                    for nb in range(2):
                        cc.ins("dve", lambda e, nb=nb:
                               e.tensor_copy(out=kfull[64:128, nb * S:(nb + 1) * S], in_=kpe_bf[:]),
                               inc="tV")
                    kpe_tok = cc.tok("tV")

                    qh_toks = [None, None]
                    qh_free = [[], []]
                    ast_free = [None, None]
                    pe_end = [None, None]
                    pkv_free = [None]
                    kv_done_k = [None, None]
                    kv_done_v = [None, None]
                    kam_done = [None, None]
                    kmbc_tok = [None, None]
                    kb2_tok = [None, None]
                    km_dma_guard = [None, None]
                    bnd_tok = [None, None]
                    bnd_guard = [None, None]
                    e_prev = [None, None]
                    pb_prev = [None, None, None]
                    l_guard = [[] for _ in range(6)]
                    lrun_guard = [None, None]
                    rec_guard = [None, None]
                    rec_rot = [0]
                    obf_free = [None, None, None, None]
                    qT_free = [None, None]
                    qT_tok = [None, None]
                    ptrP_war = [None, None]
                    ptrP_rot = [0]
                    av_prev = [None, None]
                    pacc_free = [None, None]
                    pending_o = []
                    tac_seq = [0]

                    def qh_load(h):
                        if h >= H:
                            return
                        hsl = h % 2
                        dq = "dQ0" if hsl == 0 else "dQ1"
                        for t in qh_free[hsl]:
                            cc.waitt("gp", t)
                        qsrc = bass.AP(q_sc, h * HD,
                                       [[H * HD, 128], [128 * H * HD, NO], [1, 128]])
                        cc.ins("gp", lambda e, hsl=hsl, qsrc=qsrc:
                               e.dma_start(out=bass.AP(qh, hsl * NO * 128,
                                                       [[2 * NO * 128, 128], [128, NO], [1, 128]]),
                                           in_=qsrc),
                               inc=dq, k=16)
                        qh_toks[hsl] = cc.tok(dq)

                    unit_tok = {}

                    def emit_unit(h1, u, phase=None):
                        nb = h1 % 2
                        ks = u // 2
                        if u % 2 == 0:
                            if phase == 0:
                                cc.waitt("pe", pkv_free[0])
                                for ccx in range(KVC):
                                    lo = ccx * H * HKV + h1 * HKV
                                    cc.ins("pe", lambda e, ccx=ccx, lo=lo, ks=ks:
                                           e.matmul(pex[0:64, 0:512],
                                                    wkvb_bf[:, lo:lo + 64],
                                                    c_kv[:, ccx * S + ks * 512: ccx * S + (ks + 1) * 512],
                                                    start=(ccx == 0), stop=(ccx == KVC - 1)),
                                           inc="tP" if ccx == KVC - 1 else None)
                                unit_tok[(h1, u)] = cc.tok("tP")
                                return
                            kt = unit_tok[(h1, u)]
                            cc.waitt("act", kt)
                            cc.waitt("act", pe_end[nb])
                            cc.ins("act", lambda e, nb=nb, ks=ks:
                                   e.activation(out=kfull[0:64, nb * S + ks * 512: nb * S + (ks + 1) * 512],
                                                in_=pex[0:64, 0:512], func=AF.Copy, scale=SCALE),
                                   inc="tA")
                            tokk = cc.tok("tA")
                            pkv_free[0] = tokk
                            kv_done_k[nb] = tokk
                            cc.waitt("dve", tokk)
                            cc.waitt("dve", kpe_tok)
                            cc.ins("dve", lambda e, nb=nb, ks=ks:
                                   e.tensor_tensor(out=ksq_sb[:],
                                                   in0=kfull[:, nb * S + ks * 512: nb * S + (ks + 1) * 512],
                                                   in1=kfull[:, nb * S + ks * 512: nb * S + (ks + 1) * 512],
                                                   op=ALU.mult),
                                   inc="tV")
                            cc.wait("dve", "tV", cc.state["sems"]["tV"])
                            cc.ins("dve", lambda e, nb=nb, ks=ks:
                                   e.tensor_reduce(kamb[:, nb * 8 + ks: nb * 8 + ks + 1],
                                                   ksq_sb[:], axis=AX.X, op=ALU.max),
                                   inc="tV")
                            if ks == NKS - 1:
                                cc.wait("dve", "tV", cc.state["sems"]["tV"])
                                cc.waitt("dve", km_dma_guard[nb])
                                cc.ins("dve", lambda e, nb=nb:
                                       e.tensor_reduce(kamb[:, 16 + nb: 17 + nb],
                                                       kamb[:, nb * 8: nb * 8 + 8],
                                                       axis=AX.X, op=ALU.max),
                                       inc="tV")
                                kam_done[nb] = cc.tok("tV")
                                cc.waitt("sp", kam_done[nb])
                                cc.ins("sp", lambda e, nb=nb:
                                       e.dma_start(out=bass.AP(kmsc, nb * 128, [[1, 128], [1, 1]]),
                                                   in_=kamb[:, 16 + nb: 17 + nb]),
                                       inc="dS", k=16)
                                t1 = cc.tok("dS")
                                km_dma_guard[nb] = t1
                                cc.waitt("sp", t1)
                                cc.waitt("sp", kb2_tok[nb])
                                cc.ins("sp", lambda e, nb=nb:
                                       e.dma_start(out=kmbc[:, nb * 128:(nb + 1) * 128],
                                                   in_=bass.AP(kmsc, nb * 128, [[0, 128], [1, 128]])),
                                       inc="dS", k=16)
                                kmbc_tok[nb] = cc.tok("dS")
                                cc.waitt("dve", kmbc_tok[nb])
                                cc.ins("dve", lambda e, nb=nb:
                                       e.tensor_reduce(kb2[:, nb:nb + 1],
                                                       kmbc[:, nb * 128:(nb + 1) * 128],
                                                       axis=AX.X, op=ALU.add),
                                       inc="tV")
                                kb2_tok[nb] = cc.tok("tV")
                        else:
                            if phase == 0:
                                cc.waitt("pe", pkv_free[0])
                                for kc in range(4):
                                    for ccx in range(KVC):
                                        lo = ccx * H * HKV + h1 * HKV + NOPE
                                        cc.ins("pe", lambda e, kc=kc, ccx=ccx, lo=lo, ks=ks:
                                               e.matmul(pex[:, kc * 128:(kc + 1) * 128],
                                                        c_kv[:, ccx * S + ks * 512 + kc * 128: ccx * S + ks * 512 + (kc + 1) * 128],
                                                        wkvb_bf[:, lo:lo + VD],
                                                        start=(ccx == 0), stop=(ccx == KVC - 1)),
                                               inc="tP" if (kc == 3 and ccx == KVC - 1) else None)
                                unit_tok[(h1, u)] = cc.tok("tP")
                                return
                            vt = unit_tok[(h1, u)]
                            cc.waitt("dve", vt)
                            cc.waitt("dve", pe_end[nb])
                            cc.ins("dve", lambda e, nb=nb, ks=ks:
                                   e.tensor_copy(out=v_tm[:, nb * S + ks * 512: nb * S + (ks + 1) * 512],
                                                 in_=pex[:, 0:512]),
                                   inc="tV")
                            tokv = cc.tok("tV")
                            pkv_free[0] = tokv
                            kv_done_v[nb] = tokv

                    def emit_bounds(h):
                        hsl = h % 2
                        hb = h % 2
                        cc.waitt("dve", qh_toks[hsl])
                        cc.waitt("dve", kb2_tok[hb])
                        cc.waitt("dve", bnd_guard[hsl])
                        for i in range(NO):
                            cc.wait("dve", "tV", cc.state["sems"]["tV"])
                            cc.ins("dve", lambda e, i=i:
                                   e.tensor_tensor(out=qsq_sb[:],
                                                   in0=qh[:, hsl * NO * 128 + i * 128: hsl * NO * 128 + (i + 1) * 128],
                                                   in1=qh[:, hsl * NO * 128 + i * 128: hsl * NO * 128 + (i + 1) * 128],
                                                   op=ALU.mult),
                                   inc="tV")
                            cc.wait("dve", "tV", cc.state["sems"]["tV"])
                            cc.ins("dve", lambda e, i=i:
                                   e.tensor_reduce(bnd[:, hsl * NO + i: hsl * NO + i + 1],
                                                   qsq_sb[:], axis=AX.X, op=ALU.add),
                                   inc="tV")
                        cc.wait("dve", "tV", cc.state["sems"]["tV"])
                        cc.ins("dve", lambda e:
                               e.tensor_scalar(out=bnd[:, hsl * NO: hsl * NO + NO],
                                               in0=bnd[:, hsl * NO: hsl * NO + NO],
                                               scalar1=kb2[:, hb:hb + 1], scalar2=None,
                                               op0=ALU.mult),
                               inc="tV")
                        vtok = cc.tok("tV")
                        cc.waitt("act", vtok)
                        cc.waitt("act", bnd_guard[hsl])
                        cc.ins("act", lambda e:
                               e.activation(out=bnd[:, hsl * NO: hsl * NO + NO],
                                            in_=bnd[:, hsl * NO: hsl * NO + NO],
                                            func=AF.Ln),
                               inc="tA")
                        cc.wait("act", "tA", cc.state["sems"]["tA"])
                        cc.ins("act", lambda e:
                               e.activation(out=bnd[:, hsl * NO: hsl * NO + NO],
                                            in_=bnd[:, hsl * NO: hsl * NO + NO],
                                            func=AF.Exp, scale=0.5),
                               inc="tA")
                        atok = cc.tok("tA")
                        cc.waitt("dve", atok)
                        cc.ins("dve", lambda e:
                               e.tensor_scalar(out=bnd[:, hsl * NO: hsl * NO + NO],
                                               in0=bnd[:, hsl * NO: hsl * NO + NO],
                                               scalar1=-1.05, scalar2=-1.0,
                                               op0=ALU.mult, op1=ALU.add),
                               inc="tV")
                        bnd_tok[hsl] = cc.tok("tV")
                        qh_free[hsl].append(bnd_tok[hsl])

                    def flush_o(all_=True):
                        while pending_o:
                            if not all_ and pending_o[0][4] > tac_seq[0] - 4:
                                break
                            hsl2, ob2, i2, o_tok2, _sq = pending_o.pop(0)
                            half = ptrP_rot[0] % 2
                            ptrP_rot[0] += 1
                            stg = STG[half]
                            cc.waitt("pe", o_tok2)
                            cc.waitt("pe", ptrP_war[half])
                            cc.ins("pe", lambda e, ob2=ob2, stg=stg:
                                   e.transpose(stg[:, 0:128],
                                               obf[:, ob2 * 128:(ob2 + 1) * 128],
                                               idn_b[:]), inc="tP")
                            otr = cc.tok("tP")
                            obf_free[ob2] = otr
                            cc.waitt("dve", otr)
                            cc.waitt("dve", ast_free[hsl2])
                            cc.ins("dve", lambda e, i2=i2, stg=stg, hsl2=hsl2:
                                   e.tensor_copy(out=astage[:, hsl2 * TOK + i2 * 128: hsl2 * TOK + (i2 + 1) * 128],
                                                 in_=stg[:, 0:128]),
                                   inc="tV")
                            ptrP_war[half] = cc.tok("tV")

                    def emit_qT(h, i):
                        hsl = h % 2
                        par = i % 2
                        half = ptrP_rot[0] % 2
                        ptrP_rot[0] += 1
                        stg = STG[half]
                        cc.waitt("pe", qh_toks[hsl])
                        cc.waitt("pe", ptrP_war[half])
                        cc.ins("pe", lambda e, i=i, stg=stg, hsl=hsl:
                               e.transpose(stg[:, 0:128],
                                           qh[:, hsl * NO * 128 + i * 128: hsl * NO * 128 + (i + 1) * 128],
                                           idn_b[:]), inc="tP")
                        tr = cc.tok("tP")
                        qh_free[hsl].append(tr)
                        cc.waitt("dve", tr)
                        cc.waitt("dve", qT_free[par])
                        cc.ins("dve", lambda e, stg=stg, par=par:
                               e.tensor_copy(out=qTb[:, par * 128:(par + 1) * 128],
                                             in_=stg[:, 0:128]), inc="tV")
                        qT_tok[par] = cc.tok("tV")
                        ptrP_war[half] = qT_tok[par]

                    def emit_S(jb):
                        i, s2, sk0, L = jb["i"], jb["s"], jb["sk0"], jb["L"]
                        h, hb, hsl = jb["h"], jb["hb"], jb["hsl"]
                        jp = jb["jp"]
                        po = PSOFF[jp]
                        par = i % 2
                        jb["par"] = par
                        cc.waitt("pe", e_prev[jp])
                        cc.waitt("pe", kv_done_k[hb])
                        cc.waitt("pe", kpe_tok)
                        cc.waitt("pe", qT_tok[par])
                        for k0 in range(0, L, 512):
                            w = min(512, L - k0)
                            cc.ins("pe", lambda e, po=po, k0=k0, w=w, sk0=sk0, par=par, hb=hb:
                                   e.matmul(psB[:, po + k0: po + k0 + w],
                                            qTb[:, par * 128:(par + 1) * 128],
                                            kfull[:, hb * S + sk0 + k0: hb * S + sk0 + k0 + w],
                                            start=True, stop=True),
                                   inc="tP" if k0 + w >= L else None)
                        sc_tok = cc.tok("tP")
                        qT_free[par] = sc_tok
                        exp_dep = sc_tok
                        if s2 == jb["nseg"] - 1:
                            wm = 512 if jb["E"] % 512 == 0 else jb["E"] % 512
                            cc.waitt("dve", sc_tok)
                            cc.waitt("dve", masks_tok)
                            cc.ins("dve", lambda e, po=po, L=L, wm=wm, i=i:
                                   e.tensor_tensor(out=psB[:, po + L - wm: po + L],
                                                   in0=psB[:, po + L - wm: po + L],
                                                   in1=masks_sb[:, i * 512 + 512 - wm:(i + 1) * 512],
                                                   op=ALU.add),
                                   inc="tV")
                            exp_dep = cc.tok("tV")
                        lcol = jb["gidx"] % 6
                        pb = jb["gidx"] % 3
                        cc.waitt("act", exp_dep)
                        cc.waitt("act", bnd_tok[hsl])
                        cc.waitt("act", pb_prev[pb])
                        for t in l_guard[jb["gidx"] % 6]:
                            cc.waitt("act", t)
                        l_guard[jb["gidx"] % 6] = []
                        cc.ins("act", lambda e, pb=pb, po=po, L=L, lcol=lcol, hsl=hsl, i=i:
                               e.activation(out=Pbuf[:, pb * 1024: pb * 1024 + L],
                                            in_=psB[:, po: po + L],
                                            func=AF.Exp, bias=bnd[:, hsl * NO + i: hsl * NO + i + 1],
                                            accum_out=sm[:, lcol:lcol + 1]),
                               inc="tA")
                        jb["e_tok"] = cc.tok("tA")
                        e_prev[jp] = jb["e_tok"]
                        bnd_guard[hsl] = jb["e_tok"]
                        jb["lcol"] = lcol
                        jb["pb"] = pb

                    def emit_T(jb):
                        L = jb["L"]
                        jp, pb = jb["jp"], jb["pb"]
                        tac_seq[0] += 1
                        flush_o(all_=False)
                        nch = L // 128
                        ngrp = (nch + 3) // 4
                        cc.waitt("pe", jb["e_tok"])
                        copy_toks = []
                        for g in range(ngrp):
                            gw = min(512, L - g * 512)
                            half = ptrP_rot[0] % 2
                            ptrP_rot[0] += 1
                            stg = STG[half]
                            cc.waitt("pe", ptrP_war[half])
                            for cch in range(gw // 128):
                                cix = g * 4 + cch
                                cc.ins("pe", lambda e, stg=stg, cch=cch, cix=cix, pb=pb:
                                       e.transpose(stg[:, cch * 128:(cch + 1) * 128],
                                                   Pbuf[:, pb * 1024 + cix * 128: pb * 1024 + (cix + 1) * 128],
                                                   idn_b[:]),
                                       inc="tP" if cch == gw // 128 - 1 else None)
                            gt = cc.tok("tP")
                            cc.waitt("dve", gt)
                            if g == 0:
                                cc.waitt("dve", av_prev[jp])
                            cc.ins("dve", lambda e, stg=stg, g=g, gw=gw, jp=jp:
                                   e.tensor_copy(out=PTs[:, jp * 1024 + g * 512: jp * 1024 + g * 512 + gw],
                                                 in_=stg[:, 0:gw]),
                                   inc="tV")
                            ct = cc.tok("tV")
                            copy_toks.append(ct)
                            ptrP_war[half] = ct
                        jb["copy_toks"] = copy_toks
                        jb["T_last_pe"] = cc.tok("tP")
                        pb_prev[pb] = jb["T_last_pe"]

                    def emit_AC(jb):
                        i, s2, sk0, L, nseg = jb["i"], jb["s"], jb["sk0"], jb["L"], jb["nseg"]
                        h, hb, hsl = jb["h"], jb["hb"], jb["hsl"]
                        jp, par, pb = jb["jp"], jb["par"], jb["pb"]
                        nch = L // 128
                        ngrp = (nch + 3) // 4
                        copy_toks = jb["copy_toks"]
                        if s2 == 0:
                            cc.waitt("pe", pacc_free[par])
                        cc.waitt("pe", kv_done_v[hb])
                        for g in range(ngrp):
                            gw = min(512, L - g * 512)
                            cc.waitt("pe", copy_toks[g])
                            for cch in range(gw // 128):
                                kt = (sk0 + g * 512 + cch * 128) // 128
                                fst = (s2 == 0 and g == 0 and cch == 0)
                                lst = (s2 == nseg - 1 and g == ngrp - 1 and cch == gw // 128 - 1)
                                cc.ins("pe", lambda e, par=par, g=g, cch=cch, kt=kt, jp=jp, hb=hb,
                                       fst=fst, lst=lst:
                                       e.matmul(ACCB[par][:, 0:128],
                                                PTs[:, jp * 1024 + g * 512 + cch * 128: jp * 1024 + g * 512 + (cch + 1) * 128],
                                                v_tm[:, hb * S + kt * 128: hb * S + (kt + 1) * 128],
                                                start=fst, stop=lst),
                                       inc="tP" if (g == ngrp - 1 and cch == gw // 128 - 1) else None)
                        av = cc.tok("tP")
                        av_prev[jp] = av
                        lcol = jb["lcol"]
                        if nseg > 1 and s2 >= 1:
                            prev = jb["prev"]
                            cc.waitt("gp", jb["e_tok"])
                            cc.wait("gp", "tG", cc.state["sems"]["tG"])
                            if s2 == 1:
                                cc.waitt("gp", prev["e_tok"])
                                cc.waitt("gp", lrun_guard[par])
                                cc.ins("gp", lambda e, par=par, lcol=lcol, plcol=prev["lcol"]:
                                       e.tensor_tensor(out=sm[:, 8 + par:9 + par],
                                                       in0=sm[:, plcol:plcol + 1],
                                                       in1=sm[:, lcol:lcol + 1], op=ALU.add),
                                       inc="tG")
                            else:
                                cc.ins("gp", lambda e, par=par, lcol=lcol:
                                       e.tensor_tensor(out=sm[:, 8 + par:9 + par],
                                                       in0=sm[:, 8 + par:9 + par],
                                                       in1=sm[:, lcol:lcol + 1], op=ALU.add),
                                       inc="tG")
                            l_tok = cc.tok("tG")
                            l_guard[jb["gidx"] % 6].append(l_tok)
                            if s2 == 1:
                                l_guard[prev["gidx"] % 6].append(l_tok)
                        if s2 == nseg - 1:
                            rr = rec_rot[0] % 2
                            rec_rot[0] += 1
                            rec_col = 18 + rr
                            if nseg == 1:
                                cc.waitt("dve", jb["e_tok"])
                                lsrc = lcol
                            else:
                                cc.waitt("dve", l_tok)
                                lsrc = 8 + par
                            cc.waitt("dve", rec_guard[rr])
                            cc.ins("dve", lambda e, rec_col=rec_col, lsrc=lsrc:
                                   e.reciprocal(sm[:, rec_col:rec_col + 1], sm[:, lsrc:lsrc + 1]),
                                   inc="tV")
                            rec = cc.tok("tV")
                            if nseg == 1:
                                l_guard[jb["gidx"] % 6].append(rec)
                            else:
                                lrun_guard[par] = rec
                            ob4 = i % 4
                            cc.waitt("dve", av)
                            cc.waitt("dve", obf_free[ob4])
                            cc.wait("dve", "tV", cc.state["sems"]["tV"])
                            cc.ins("dve", lambda e, ob4=ob4, par=par, rec_col=rec_col:
                                   e.tensor_scalar(out=obf[:, ob4 * 128:(ob4 + 1) * 128],
                                                   in0=ACCB[par][:, 0:128],
                                                   scalar1=sm[:, rec_col:rec_col + 1], scalar2=None,
                                                   op0=ALU.mult),
                                   inc="tV")
                            o_tok = cc.tok("tV")
                            rec_guard[rr] = o_tok
                            pacc_free[par] = o_tok
                            pending_o.append((hsl, ob4, i, o_tok, tac_seq[0]))
                        if jb.get("last_of_head"):
                            flush_o()
                            pe_end[hb] = cc.tok("tP")
                            qh_free[hsl].append(pe_end[hb])
                            fin = cc.tok("tV")
                            cc.waitt("gp", fin)
                            da = "dA0" if hsl == 0 else "dA1"
                            cc.ins("gp", lambda e, h=h, hsl=hsl:
                                   e.dma_start(out=attn_a[h], in_=astage[:, hsl * TOK:(hsl + 1) * TOK]),
                                   inc=da, k=16)
                            ast_free[hsl] = cc.tok(da)

                    gjobs = []
                    head_start = {}
                    fill_map = {}
                    prep_map = {}
                    bmap = {}
                    for h in range(H):
                        head_start[len(gjobs)] = h
                        base = len(gjobs)
                        first_job = {}
                        for i in range(NO):
                            E = 256 * (i + 1)
                            first_job[i] = len(gjobs)
                            sk0 = 0
                            tj = []
                            for_prev = None
                            while sk0 < E:
                                jp = len(gjobs) % 2
                                L = min(PSCAP[jp], E - sk0)
                                jb = dict(i=i, s=len(tj), sk0=sk0, L=L, E=E, jp=jp,
                                          h=h, hb=h % 2, hsl=h % 2, gidx=len(gjobs),
                                          prev=for_prev)
                                gjobs.append(jb)
                                tj.append(jb)
                                for_prev = jb
                                sk0 += L
                            for jb2 in tj:
                                jb2["nseg"] = len(tj)
                        gjobs[-1]["last_of_head"] = True
                        nj = len(gjobs) - base
                        if h + 1 < H:
                            uorder = list(range(0, 2 * NKS, 2)) + list(range(1, 2 * NKS, 2))
                            for slot, u in enumerate(uorder):
                                pos = base + 2 + (slot * (nj - 3)) // (2 * NKS)
                                fill_map.setdefault(pos, []).append((h + 1, u, 0))
                                fill_map.setdefault(pos + 1, []).append((h + 1, u, 1))
                        for i in range(1, NO):
                            prep_map.setdefault(first_job[i - 1], []).append((h, i))
                        if h + 1 < H:
                            prep_map.setdefault(first_job[NO - 1], []).append((h + 1, 0))
                            bmap.setdefault(base + (2 * nj) // 3, []).append(h + 1)

                    qh_load(0)
                    for u in range(2 * NKS):
                        emit_unit(0, u, 0)
                        emit_unit(0, u, 1)
                    emit_bounds(0)
                    emit_qT(0, 0)
                    NG = len(gjobs)
                    for g in range(NG):
                        if g in head_start:
                            h = head_start[g]
                            qh_load(h + 1)
                        for h1 in bmap.get(g, []):
                            emit_bounds(h1)
                        for (h1, u, ph2) in fill_map.get(g, []):
                            emit_unit(h1, u, ph2)
                        for (hh, ip) in prep_map.get(g, []):
                            emit_qT(hh, ip)
                        emit_S(gjobs[g])
                        if g >= 2:
                            emit_T(gjobs[g - 2])
                        if g >= 3:
                            emit_AC(gjobs[g - 3])
                    emit_T(gjobs[NG - 2])
                    emit_AC(gjobs[NG - 3])
                    emit_T(gjobs[NG - 1])
                    emit_AC(gjobs[NG - 2])
                    emit_AC(gjobs[NG - 1])
                    flush_o()

                c.run_phase(ph_b)
            c.full_gate()

        for hf in range(2):
            with ExitStack() as stCD:
                ps4 = stCD.enter_context(nc.psum_tensor(f"psCD{hf}", [128, 2048], F32))
                x2 = stCD.enter_context(nc.sbuf_tensor(f"x2_{hf}", [128, DC * TH], F32))

                with ExitStack() as stC:
                    wo_bf = stC.enter_context(nc.sbuf_tensor(f"wo_bf{hf}", [128, (H * VD // 128) * D], BF16))
                    wstC = stC.enter_context(nc.sbuf_tensor(f"wstC{hf}", [128, 2 * 2048], F32))
                    attn_res = stC.enter_context(nc.sbuf_tensor(f"attn_res{hf}", [128, H * TH], BF16))
                    xringC = stC.enter_context(nc.sbuf_tensor(f"xringC{hf}", [128, 2 * D], F32))

                    ICC = H * VD // 128

                    def ph_c(cc, hf=hf):
                        xr_free = [None, None]
                        bank_free = [None, None]
                        ps_free = [None, None]
                        for ii in range(NTH):
                            i = hf * NTH + ii
                            sl = ii % 2
                            dx = "dX0" if sl == 0 else "dX1"
                            cc.waitt("gp", xr_free[sl])
                            cc.ins("gp", lambda e, i=i, sl=sl:
                                   e.dma_start(out=xringC[:, sl * D:(sl + 1) * D], in_=xa[2 * i]),
                                   inc=dx, k=16)
                            xl = cc.tok(dx)
                            cc.waitt("pe", xl)
                            for r in range(DC // 4):
                                pb = (pex, pvb)[r % 2]
                                cc.waitt("pe", bank_free[r % 2])
                                for j in range(4):
                                    dcx = r * 4 + j
                                    cc.ins("pe", lambda e, sl=sl, dcx=dcx, j=j, pb=pb:
                                           e.transpose(pb[:, j * 128:(j + 1) * 128],
                                                       xringC[:, sl * D + dcx * 128: sl * D + (dcx + 1) * 128],
                                                       idn_f[:]),
                                           inc="tP" if j == 3 else None)
                                tr = cc.tok("tP")
                                cc.waitt("dve", tr)
                                for j in range(4):
                                    dcx = r * 4 + j
                                    cc.ins("dve", lambda e, dcx=dcx, ii=ii, j=j, pb=pb:
                                           e.tensor_copy(out=x2[:, dcx * TH + ii * 128: dcx * TH + (ii + 1) * 128],
                                                         in_=pb[:, j * 128:(j + 1) * 128]),
                                           inc="tV" if j == 3 else None)
                                bank_free[r % 2] = cc.tok("tV")
                            xr_free[sl] = cc.tok("tV")
                        x2_ready = cc.tok("tV")

                        cast_tok = [None] * ICC
                        for icx in range(ICC):
                            slot = icx % 2
                            if icx >= 2:
                                cc.waitt("sp", cast_tok[icx - 2])
                            dw = "dW0" if slot == 0 else "dW1"
                            cc.ins("sp", lambda e, icx=icx, slot=slot:
                                   e.dma_start(out=wstC[:, slot * 2048: slot * 2048 + D],
                                               in_=wo_d.ap()[icx * 128:(icx + 1) * 128, :]),
                                   inc=dw, k=16)
                            cc.wait("dve", dw, cc.state["sems"][dw])
                            cast_tok[icx] = cc.ins("dve", lambda e, icx=icx, slot=slot:
                                                   e.tensor_copy(out=wo_bf[:, icx * D:(icx + 1) * D],
                                                                 in_=wstC[:, slot * 2048: slot * 2048 + D]),
                                                   inc="tV")
                        wo_ready = cast_tok[-1]

                        for icx in range(ICC):
                            cc.ins("sp", lambda e, icx=icx, hf=hf:
                                   e.dma_start(out=attn_res[:, icx * TH:(icx + 1) * TH],
                                               in_=attn_a[icx][:, hf * TH:(hf + 1) * TH]),
                                   inc="dS", k=16)
                        attn_tok = cc.tok("dS")

                        nsl = (TH + 511) // 512
                        for oc in range(DC):
                            pbk = oc % 2
                            cc.waitt("pe", ps_free[pbk])
                            cc.waitt("pe", wo_ready)
                            cc.waitt("pe", attn_tok)
                            for icx in range(ICC):
                                for s in range(nsl):
                                    w = min(512, TH - s * 512)
                                    cc.ins("pe", lambda e, pbk=pbk, icx=icx, s=s, w=w, oc=oc,
                                           first=(icx == 0), last=(icx == ICC - 1):
                                           e.matmul(ps4[:, pbk * 1024 + s * 512: pbk * 1024 + s * 512 + w],
                                                    wo_bf[:, icx * D + oc * 128: icx * D + oc * 128 + 128],
                                                    attn_res[:, icx * TH + s * 512: icx * TH + s * 512 + w],
                                                    start=first, stop=last),
                                           inc="tP" if (icx == ICC - 1 and s == nsl - 1) else None)
                            grp = cc.tok("tP")
                            cc.waitt("dve", grp)
                            cc.waitt("dve", x2_ready)
                            cc.ins("dve", lambda e, pbk=pbk, oc=oc:
                                   e.tensor_tensor(out=x2[:, oc * TH:(oc + 1) * TH],
                                                   in0=ps4[:, pbk * 1024: pbk * 1024 + TH],
                                                   in1=x2[:, oc * TH:(oc + 1) * TH], op=ALU.add),
                                   inc="tV")
                            ps_free[pbk] = cc.tok("tV")

                    c.run_phase(ph_c)
                c.full_gate()

                with ExitStack() as stD:
                    xn2 = stD.enter_context(nc.sbuf_tensor(f"xn2_{hf}", [128, DC * TH], BF16))
                    gated = stD.enter_context(nc.sbuf_tensor(f"gated{hf}", [128, SUPC * TH], BF16))
                    wg_bf = stD.enter_context(nc.sbuf_tensor(f"wg_bf{hf}", [128, 2 * DC * SUP], BF16))
                    wu_bf = stD.enter_context(nc.sbuf_tensor(f"wu_bf{hf}", [128, 2 * DC * SUP], BF16))
                    wd_bf = stD.enter_context(nc.sbuf_tensor(f"wd_bf{hf}", [128, SUPC * D], BF16))
                    WSLOT = max(D, 4 * SUP)
                    wtmp_cols = max(2 * WSLOT, 3 * TH)
                    wtmp = stD.enter_context(nc.sbuf_tensor(f"wtmp{hf}", [128, wtmp_cols], F32))
                    silu_t = stD.enter_context(nc.sbuf_tensor(f"silu{hf}", [128, max(2 * 512, TH)], F32))

                    def ph_d(cc, hf=hf):
                        nsl = (TH + 511) // 512
                        rows = wtmp
                        for dcx in range(DC):
                            cc.ins("gp", lambda e, dcx=dcx:
                                   e.tensor_copy(out=xn2[:, dcx * TH:(dcx + 1) * TH],
                                                 in_=x2[:, dcx * TH:(dcx + 1) * TH]),
                                   inc="tG" if dcx == DC - 1 else None)
                        x2b = cc.tok("tG")
                        cc.waitt("pe", x2b)
                        for s in range(nsl):
                            w = min(512, TH - s * 512)
                            for dcx in range(DC):
                                cc.ins("pe", lambda e, s=s, w=w, dcx=dcx,
                                       first=(dcx == 0), last=(dcx == DC - 1):
                                       e.matmul(ps4[0:1, s * 512: s * 512 + w],
                                                ones_neg[:],
                                                xn2[:, dcx * TH + s * 512: dcx * TH + s * 512 + w],
                                                start=first, stop=last),
                                       inc="tP" if dcx == DC - 1 else None)
                        sneg = cc.tok("tP")
                        cc.waitt("gp", sneg)
                        for dcx in range(DC):
                            cc.ins("gp", lambda e, dcx=dcx:
                                   e.tensor_tensor(out=xn2[:, dcx * TH:(dcx + 1) * TH],
                                                   in0=xn2[:, dcx * TH:(dcx + 1) * TH],
                                                   in1=xn2[:, dcx * TH:(dcx + 1) * TH], op=ALU.mult),
                                   inc="tG" if dcx == DC - 1 else None)
                        sq_done = cc.tok("tG")
                        cc.waitt("pe", sq_done)
                        for s in range(nsl):
                            w = min(512, TH - s * 512)
                            for dcx in range(DC):
                                cc.ins("pe", lambda e, s=s, w=w, dcx=dcx,
                                       first=(dcx == 0), last=(dcx == DC - 1):
                                       e.matmul(ps4[0:1, 1024 + s * 512: 1024 + s * 512 + w],
                                                ones_inv[:],
                                                xn2[:, dcx * TH + s * 512: dcx * TH + s * 512 + w],
                                                start=first, stop=last),
                                       inc="tP" if dcx == DC - 1 else None)
                        ssq = cc.tok("tP")
                        cc.waitt("dve", sneg)
                        cc.ins("dve", lambda e: e.tensor_copy(out=rows[0:1, 0:TH], in_=ps4[0:1, 0:TH]), inc="tV")
                        cc.waitt("dve", ssq)
                        cc.ins("dve", lambda e: e.tensor_copy(out=rows[0:1, TH:2 * TH], in_=ps4[0:1, 1024:1024 + TH]), inc="tV")
                        cc.wait("dve", "tV", cc.state["sems"]["tV"])
                        cc.ins("dve", lambda e:
                               e.tensor_tensor(out=rows[0:1, 2 * TH:3 * TH], in0=rows[0:1, 0:TH],
                                               in1=rows[0:1, 0:TH], op=ALU.mult), inc="tV")
                        cc.wait("dve", "tV", cc.state["sems"]["tV"])
                        cc.ins("dve", lambda e:
                               e.tensor_tensor(out=rows[0:1, TH:2 * TH], in0=rows[0:1, TH:2 * TH],
                                               in1=rows[0:1, 2 * TH:3 * TH], op=ALU.subtract), inc="tV")
                        var_tok = cc.tok("tV")
                        cc.waitt("act", var_tok)
                        cc.ins("act", lambda e:
                               e.activation(out=rows[0:1, 2 * TH:3 * TH], in_=rows[0:1, TH:2 * TH],
                                            func=AF.Ln, bias=epsb[0:1, 0:1]), inc="tA")
                        cc.wait("act", "tA", cc.state["sems"]["tA"])
                        cc.ins("act", lambda e:
                               e.activation(out=rows[0:1, 2 * TH:3 * TH], in_=rows[0:1, 2 * TH:3 * TH],
                                            func=AF.Exp, scale=-0.5), inc="tA")
                        rstd_tok = cc.tok("tA")
                        cc.waitt("pe", rstd_tok)
                        for s in range(nsl):
                            w = min(512, TH - s * 512)
                            cc.ins("pe", lambda e, s=s, w=w:
                                   e.matmul(ps4[:, s * 512: s * 512 + w], ones_row[:],
                                            rows[0:1, s * 512: s * 512 + w], start=True, stop=True),
                                   inc="tP")
                            cc.ins("pe", lambda e, s=s, w=w:
                                   e.matmul(ps4[:, 1024 + s * 512: 1024 + s * 512 + w], ones_row[:],
                                            rows[0:1, 2 * TH + s * 512: 2 * TH + s * 512 + w],
                                            start=True, stop=True),
                                   inc="tP")
                        bc_tok = cc.tok("tP")
                        cc.waitt("dve", bc_tok)
                        for dcx in range(DC):
                            cc.ins("dve", lambda e, dcx=dcx:
                                   e.tensor_tensor(out=silu_t[:, 0:TH], in0=x2[:, dcx * TH:(dcx + 1) * TH],
                                                   in1=ps4[:, 0:TH], op=ALU.add), inc="tV")
                            cc.wait("dve", "tV", cc.state["sems"]["tV"])
                            cc.ins("dve", lambda e, dcx=dcx:
                                   e.tensor_tensor(out=xn2[:, dcx * TH:(dcx + 1) * TH],
                                                   in0=silu_t[:, 0:TH], in1=ps4[:, 1024:1024 + TH],
                                                   op=ALU.mult), inc="tV")
                            cc.wait("dve", "tV", cc.state["sems"]["tV"])
                        xn2_tok = cc.tok("tV")

                        st_free = [bc_tok, bc_tok, bc_tok, None]
                        pj = [0]
                        gu_tok = [None, None]
                        wd_tok_box = [None]
                        wgu_free = [None, None]
                        wd_free = [None]

                        def piece(src_ap, dst, doff, war):
                            slot = pj[0] % 4
                            pj[0] += 1
                            dw = ("dW0", "dW1", "dW2", "dW3")[slot]
                            cc.waitt("sp", st_free[slot])
                            cc.ins("sp", lambda e, src_ap=src_ap, slot=slot:
                                   e.dma_start(out=bass.AP(wtmp, slot * 1024,
                                                           [[wtmp_cols, 128], [SUP, 2], [1, SUP]]),
                                               in_=src_ap), inc=dw, k=16)
                            cc.wait("gp", dw, cc.state["sems"][dw])
                            cc.waitt("gp", war)
                            tokc = cc.ins("gp", lambda e, slot=slot, dst=dst, doff=doff:
                                          e.tensor_copy(out=dst[:, doff: doff + 1024],
                                                        in_=wtmp[:, slot * 1024: slot * 1024 + 1024]),
                                          inc="tG")
                            st_free[slot] = tokc
                            return tokc

                        def load_gu(s):
                            sb = s % 2
                            tokc = None
                            for wsrc, wdst in ((wgT_d, wg_bf), (wuT_d, wu_bf)):
                                for g2 in range(DC // 2):
                                    src = bass.AP(wsrc, (2 * g2 * 128) * F + s * SUP,
                                                  [[F, 128], [128 * F, 2], [1, SUP]])
                                    tokc = piece(src, wdst, sb * DC * SUP + 2 * g2 * SUP,
                                                 wgu_free[sb])
                            gu_tok[sb] = tokc

                        def load_wd(s):
                            tokc = None
                            for hcx in range(SUPC):
                                for h2 in range(D // 1024):
                                    src = bass.AP(wdT_d, (s * SUP + hcx * 128) * D + h2 * 1024,
                                                  [[D, 128], [1, 1024]])
                                    tokc = piece(src, wd_bf, hcx * D + h2 * 1024, wd_free[0])
                            wd_tok_box[0] = tokc

                        silu_free = [None, None]
                        gbank = [None, None, None, None]
                        dpsf = [None, None]
                        gated_free = None

                        load_gu(0)
                        load_wd(0)
                        for sp_i in range(NSUP):
                            sb = sp_i % 2
                            wd_tok = wd_tok_box[0]
                            if sp_i + 1 < NSUP:
                                load_gu(sp_i + 1)
                            for hcx in range(SUPC):
                                for s in range(nsl):
                                    w = min(512, TH - s * 512)
                                    gb = s % 2
                                    ub = 2 + s % 2
                                    cc.waitt("pe", gbank[gb])
                                    cc.waitt("pe", gbank[ub])
                                    cc.waitt("pe", gu_tok[sb])
                                    cc.waitt("pe", xn2_tok)
                                    for dcx in range(DC):
                                        cc.ins("pe", lambda e, gb=gb, w=w, s=s, dcx=dcx, hcx=hcx, sb=sb,
                                               first=(dcx == 0), last=(dcx == DC - 1):
                                               e.matmul(ps4[:, gb * 512: gb * 512 + w],
                                                        wg_bf[:, sb * DC * SUP + dcx * SUP + hcx * 128: sb * DC * SUP + dcx * SUP + (hcx + 1) * 128],
                                                        xn2[:, dcx * TH + s * 512: dcx * TH + s * 512 + w],
                                                        start=first, stop=last),
                                               inc="tP" if dcx == DC - 1 else None)
                                    g_grp = cc.tok("tP")
                                    for dcx in range(DC):
                                        cc.ins("pe", lambda e, ub=ub, w=w, s=s, dcx=dcx, hcx=hcx, sb=sb,
                                               first=(dcx == 0), last=(dcx == DC - 1):
                                               e.matmul(ps4[:, ub * 512: ub * 512 + w],
                                                        wu_bf[:, sb * DC * SUP + dcx * SUP + hcx * 128: sb * DC * SUP + dcx * SUP + (hcx + 1) * 128],
                                                        xn2[:, dcx * TH + s * 512: dcx * TH + s * 512 + w],
                                                        start=first, stop=last),
                                               inc="tP" if dcx == DC - 1 else None)
                                    u_grp = cc.tok("tP")
                                    cc.waitt("act", g_grp)
                                    cc.waitt("act", silu_free[s % 2])
                                    cc.ins("act", lambda e, gb=gb, w=w, s=s:
                                           e.activation(out=silu_t[:, (s % 2) * 512:(s % 2) * 512 + w],
                                                        in_=ps4[:, gb * 512: gb * 512 + w],
                                                        func=AF.Silu), inc="tA")
                                    sil = cc.tok("tA")
                                    gbank[gb] = sil
                                    cc.waitt("dve", sil)
                                    cc.waitt("dve", u_grp)
                                    if sp_i > 0 or hcx > 0 or s > 0:
                                        cc.waitt("dve", gated_free)
                                    cc.ins("dve", lambda e, ub=ub, w=w, hcx=hcx, s=s:
                                           e.tensor_tensor(out=gated[:, hcx * TH + s * 512: hcx * TH + s * 512 + w],
                                                           in0=silu_t[:, (s % 2) * 512:(s % 2) * 512 + w],
                                                           in1=ps4[:, ub * 512: ub * 512 + w], op=ALU.mult),
                                           inc="tV")
                                    gbank[ub] = cc.tok("tV")
                                    silu_free[s % 2] = gbank[ub]
                            gated_tok = cc.tok("tV")
                            wgu_free[sb] = cc.tok("tP")
                            cc.waitt("pe", gated_tok)
                            cc.waitt("pe", wd_tok)
                            for dcx in range(DC):
                                for s in range(nsl):
                                    w = min(512, TH - s * 512)
                                    pbi = (dcx * nsl + s) % 2
                                    pb = (pex, pvb)[pbi]
                                    cc.waitt("pe", dpsf[pbi])
                                    for hcx in range(SUPC):
                                        cc.ins("pe", lambda e, pb=pb, w=w, dcx=dcx, s=s, hcx=hcx,
                                               first=(hcx == 0), last=(hcx == SUPC - 1):
                                               e.matmul(pb[:, 0:w],
                                                        wd_bf[:, hcx * D + dcx * 128: hcx * D + (dcx + 1) * 128],
                                                        gated[:, hcx * TH + s * 512: hcx * TH + s * 512 + w],
                                                        start=first, stop=last),
                                               inc="tP" if hcx == SUPC - 1 else None)
                                    dgrp = cc.tok("tP")
                                    cc.waitt("dve", dgrp)
                                    cc.ins("dve", lambda e, pb=pb, w=w, dcx=dcx, s=s:
                                           e.tensor_tensor(out=x2[:, dcx * TH + s * 512: dcx * TH + s * 512 + w],
                                                           in0=pb[:, 0:w],
                                                           in1=x2[:, dcx * TH + s * 512: dcx * TH + s * 512 + w],
                                                           op=ALU.add),
                                           inc="tV")
                                    dpsf[pbi] = cc.tok("tV")
                            gated_free = cc.tok("tP")
                            wd_free[0] = cc.tok("tP")
                            if sp_i + 1 < NSUP:
                                load_wd(sp_i + 1)
                        ffn_done = cc.tok("tV")

                        ost_free = [[st_free[0], st_free[1]], [st_free[2], st_free[3]]]
                        bank_free2 = [None, None]
                        for ii in range(NTH):
                            i = hf * NTH + ii
                            osl = ii % 2
                            cc.waitt("pe", ffn_done)
                            for r in range(DC // 4):
                                pb = (pex, pvb)[r % 2]
                                cc.waitt("pe", bank_free2[r % 2])
                                for j in range(4):
                                    dcx = r * 4 + j
                                    cc.ins("pe", lambda e, dcx=dcx, ii=ii, j=j, pb=pb:
                                           e.transpose(pb[:, j * 128:(j + 1) * 128],
                                                       x2[:, dcx * TH + ii * 128: dcx * TH + (ii + 1) * 128],
                                                       idn_f[:]),
                                           inc="tP" if j == 3 else None)
                                tr = cc.tok("tP")
                                cc.waitt("dve", tr)
                                for _t in ost_free[osl]:
                                    cc.waitt("dve", _t)
                                cc.ins("dve", lambda e, r=r, osl=osl, pb=pb:
                                       e.tensor_copy(out=wtmp[:, osl * WSLOT + r * 512: osl * WSLOT + (r + 1) * 512],
                                                     in_=pb[:, 0:512]),
                                       inc="tV")
                                bank_free2[r % 2] = cc.tok("tV")
                            ev = cc.tok("tV")
                            cc.waitt("sp", ev)
                            dy = "dY0" if osl == 0 else "dY1"
                            cc.ins("sp", lambda e, i=i, osl=osl:
                                   e.dma_start(out=ya[i], in_=wtmp[:, osl * WSLOT: osl * WSLOT + D]),
                                   inc=dy, k=16)
                            ost_free[osl] = [cc.tok(dy)]

                    c.run_phase(ph_d)
                c.full_gate()

        def ph_fin(cc):
            for e in ENGINES:
                for snm in list(cc.state["sems"].keys()):
                    if snm.startswith("d"):
                        cc.wait(e, snm, cc.state["sems"][snm])

        c.run_phase(ph_fin)

    return nc



def rope_tables(S_, dtype=np.float32):
    half = ROPE // 2
    inv = 1.0 / (10000.0 ** (np.arange(half, dtype=np.float64) * 2.0 / ROPE))
    fr = np.arange(S_, dtype=np.float64)[:, None] * inv[None, :]
    cos = np.cos(fr).astype(np.float32)
    sin = np.sin(fr).astype(np.float32)
    cosI = np.repeat(cos, 2, axis=1)
    sinI = np.empty((S_, ROPE), np.float32)
    sinI[:, 0::2] = -sin
    sinI[:, 1::2] = sin
    return cosI, sinI


def host_core_inputs(cfg, xb, weights, p):
    S, D = cfg["S"], cfg["D"]
    NT, NO = S // 128, S // 256
    gl = np.empty(NT, np.int64)
    gl[0::2] = 2 * np.arange(NO) + p
    gl[1::2] = 2 * np.arange(NO) + (1 - p)
    xp = np.ascontiguousarray(xb.reshape(NT, 128, D)[gl].reshape(S, D))

    cosI, sinI = rope_tables(S)
    ck = cosI.reshape(NT, 128, ROPE)[gl]
    sk = sinI.reshape(NT, 128, ROPE)[gl]
    SCALE = float(HD) ** -0.5
    ck_sb = np.ascontiguousarray((ck * SCALE).transpose(1, 0, 2).reshape(128, NT * ROPE))
    sk_sb = np.ascontiguousarray((sk * SCALE).transpose(1, 0, 2).reshape(128, NT * ROPE))
    own = 2 * np.arange(NO) + p
    cq = cosI.reshape(NT, 128, ROPE)[own]
    sq = sinI.reshape(NT, 128, ROPE)[own]
    cq_sb = np.ascontiguousarray(cq.transpose(1, 0, 2).reshape(128, NO * ROPE))
    sq_sb = np.ascontiguousarray(sq.transpose(1, 0, 2).reshape(128, NO * ROPE))

    masks = np.zeros((NO, 128, 512), np.float32)
    for i in range(NO):
        E = 256 * (i + 1)
        w = 512 if E % 512 == 0 else E % 512
        q_glob = 128 * (2 * i + p) + np.arange(128)
        kslot = np.arange(E - w, E)
        ktile = kslot // 128
        k_glob = 128 * gl[ktile] + (kslot % 128)
        m = np.where(k_glob[None, :] > q_glob[:, None], MASK_NEG, 0.0)
        masks[i, :, 512 - w:] = m
        masks[i, :, :512 - w] = 0.0
    masks_sb = np.ascontiguousarray(masks.transpose(1, 0, 2).reshape(128, NO * 512)).astype(ml_dtypes.bfloat16)

    d = dict(weights)
    d["x"] = xp
    d["cosq"] = cq_sb
    d["sinq"] = sq_sb
    d["cosk"] = ck_sb
    d["sink"] = sk_sb
    d["masks"] = masks_sb
    d["idnf"] = np.eye(128, dtype=np.float32)
    d["idnb"] = np.eye(128, dtype=np.float32).astype(ml_dtypes.bfloat16)
    return d, gl, own


_prog_cache = {}


def kernel(L0_attn_hidden_states, L0_attn_Wqa, L0_attn_Wqb, L0_attn_Wkva,
           L0_attn_Wkvb, L0_attn_Wo, L0_ffn_w_gate, L0_ffn_w_up, L0_ffn_w_down):
    cfg = CFG_FULL
    S, D = cfg["S"], cfg["D"]
    B = L0_attn_hidden_states.shape[0]
    NO = S // 256

    key = "full"
    if key not in _prog_cache:
        _prog_cache[key] = build_program(cfg)
    nc = _prog_cache[key]

    weights = {
        "wqa": np.ascontiguousarray(L0_attn_Wqa, np.float32),
        "wqb": np.ascontiguousarray(L0_attn_Wqb, np.float32),
        "wkva": np.ascontiguousarray(L0_attn_Wkva, np.float32),
        "wkvb": np.ascontiguousarray(L0_attn_Wkvb, np.float32),
        "wo": np.ascontiguousarray(L0_attn_Wo, np.float32),
        "wgT": np.ascontiguousarray(np.asarray(L0_ffn_w_gate, np.float32).T),
        "wuT": np.ascontiguousarray(np.asarray(L0_ffn_w_up, np.float32).T),
        "wdT": np.ascontiguousarray(np.asarray(L0_ffn_w_down, np.float32).T),
    }
    x_full = np.asarray(L0_attn_hidden_states, np.float32)

    in_maps = []
    owns = []
    for core in range(8):
        b, p = core // 2, core % 2
        d, gl, own = host_core_inputs(cfg, x_full[b], weights, p)
        in_maps.append(d)
        owns.append(own)

    res = run_bass_kernel_spmd(nc, in_maps, core_ids=list(range(8)))

    out = np.empty((B, S, D), np.float32)
    for core in range(8):
        b, p = core // 2, core % 2
        y = res.results[core]["y"].reshape(NO, 128, D)
        out[b].reshape(S // 128, 128, D)[owns[core]] = y
    return out


# version 22 (final)
# speedup vs baseline: 1.0835x; 1.0835x over previous
import numpy as np
import ml_dtypes
import copy
from contextlib import ExitStack

import concourse.bass as bass
import concourse.mybir as mybir
from concourse.bass_utils import run_bass_kernel_spmd

F32 = mybir.dt.float32
BF16 = mybir.dt.bfloat16
AF = mybir.ActivationFunctionType
ALU = mybir.AluOpType
AX = mybir.AxisListType

MASK_NEG = -30000.0
EPS = 1e-5

CFG_FULL = dict(S=4096, D=2048, QL=1536, KVL=512, H=16, F=8192)
CFG_SMALL = dict(S=1024, D=512, QL=256, KVL=256, H=2, F=1024)

ROPE = 64
NOPE = 64
VD = 128
HD = NOPE + ROPE
HKV = NOPE + VD
SEG = 2048

ENGINES = ("pe", "dve", "act", "gp", "sp")


def seg_list(E):
    segs = []
    rem = E
    while rem > SEG:
        segs.append(SEG)
        rem -= SEG
    segs.append(rem)
    return segs


def slab_list(width):
    out = []
    k = 0
    while k < width:
        w = min(512, width - k)
        out.append((k, w))
        k += w
    return out


def swap_view(ap_2d, n):
    return bass.AP(ap_2d.tensor, ap_2d.offset + 1,
                   [list(ap_2d.ap[0]), [2, n // 2], [-1, 2]])


class Ctx:

    def __init__(self, block, handles):
        self.block = block
        self.handles = handles
        self.state = {"sems": {n: 0 for n in handles},
                      "wm": {e: {} for e in ENGINES}}
        self.cur = None
        self.eng = None

    def snap(self):
        return copy.deepcopy(self.state)

    def restore(self, s):
        self.state = copy.deepcopy(s)

    def wait(self, engine, sem, val):
        if val <= 0:
            return
        wm = self.state["wm"][engine]
        if wm.get(sem, 0) >= val:
            return
        wm[sem] = val
        if self.cur == engine:
            self.eng.wait_ge(self.handles[sem], val)

    def waitt(self, engine, tok):
        if tok is not None:
            self.wait(engine, tok[0], tok[1])

    def ins(self, engine, fn, inc=None, k=1):
        if inc is not None:
            self.state["sems"][inc] += k
        if self.cur == engine:
            bi = fn(self.eng)
            if inc is not None:
                bi.then_inc(self.handles[inc], k)
        return (inc, self.state["sems"][inc]) if inc is not None else None

    def tok(self, sem):
        return (sem, self.state["sems"][sem])

    def run_phase(self, phase_fn):
        s0 = self.snap()
        calls = [("pe", self.block.tensor), ("dve", self.block.vector),
                 ("act", self.block.scalar), ("gp", self.block.gpsimd),
                 ("sp", self.block.sync)]
        for name, bc in calls:
            self.restore(s0)

            def body(eng, name=name):
                self.cur = name
                self.eng = eng
                phase_fn(self)
                self.cur = None
                self.eng = None

            bc(body)

    def full_gate(self):
        def g(c):
            for e in ENGINES:
                for sem, v in list(c.state["sems"].items()):
                    c.wait(e, sem, v)
        self.run_phase(g)


def build_program(cfg):
    S, D, QL, KVL, H, F = cfg["S"], cfg["D"], cfg["QL"], cfg["KVL"], cfg["H"], cfg["F"]
    KVOUT = KVL + ROPE
    DC, QLC, KVC, FC = D // 128, QL // 128, KVL // 128, F // 128
    NT, NO = S // 128, S // 256
    TOK = NO * 128
    TH = TOK // 2
    NTH = TH // 128
    SCALE = float(HD) ** -0.5
    SUP = 512
    NSUP = F // SUP
    SUPC = SUP // 128

    nc = bass.Bass("TRN2", target_bir_lowering=False)

    x_d = nc.dram_tensor("x", [S, D], F32, kind="ExternalInput")
    wqa_d = nc.dram_tensor("wqa", [D, QL], F32, kind="ExternalInput")
    wqb_d = nc.dram_tensor("wqb", [QL, H * HD], F32, kind="ExternalInput")
    wkva_d = nc.dram_tensor("wkva", [D, KVOUT], F32, kind="ExternalInput")
    wkvb_d = nc.dram_tensor("wkvb", [KVL, H * HKV], F32, kind="ExternalInput")
    wo_d = nc.dram_tensor("wo", [H * VD, D], F32, kind="ExternalInput")
    wgT_d = nc.dram_tensor("wgT", [D, F], F32, kind="ExternalInput")
    wuT_d = nc.dram_tensor("wuT", [D, F], F32, kind="ExternalInput")
    wdT_d = nc.dram_tensor("wdT", [F, D], F32, kind="ExternalInput")
    idnf_d = nc.dram_tensor("idnf", [128, 128], F32, kind="ExternalInput")
    idnb_d = nc.dram_tensor("idnb", [128, 128], BF16, kind="ExternalInput")
    cosq_d = nc.dram_tensor("cosq", [128, NO * ROPE], F32, kind="ExternalInput")
    sinq_d = nc.dram_tensor("sinq", [128, NO * ROPE], F32, kind="ExternalInput")
    cosk_d = nc.dram_tensor("cosk", [128, NT * ROPE], F32, kind="ExternalInput")
    sink_d = nc.dram_tensor("sink", [128, NT * ROPE], F32, kind="ExternalInput")
    masks_d = nc.dram_tensor("masks", [128, NO * 512], BF16, kind="ExternalInput")
    y_d = nc.dram_tensor("y", [TOK, D], F32, kind="ExternalOutput")
    qlat_sc = nc.dram_tensor("qlat_sc", [NO, 128, QL], BF16, kind="Internal")
    q_sc = nc.dram_tensor("q_sc", [NO, 128, H * HD], BF16, kind="Internal")
    attn_sc = nc.dram_tensor("attn_sc", [H, 128, TOK], BF16, kind="Internal")
    kmsc = nc.dram_tensor("kmsc", [2, 128], BF16, kind="Internal")

    xa = x_d.ap().rearrange("(t r) d -> t r d", r=128)
    ya = y_d.ap().rearrange("(t r) d -> t r d", r=128)
    qlat_a = qlat_sc.ap()
    q_a = q_sc.ap()
    attn_a = attn_sc.ap()

    with ExitStack() as st:
        idn_f = st.enter_context(nc.sbuf_tensor("idn_f", [128, 128], F32))
        idn_b = st.enter_context(nc.sbuf_tensor("idn_b", [128, 128], BF16))
        ones_neg = st.enter_context(nc.sbuf_tensor("ones_neg", [128, 1], BF16))
        ones_inv = st.enter_context(nc.sbuf_tensor("ones_inv", [128, 1], BF16))
        ones_row = st.enter_context(nc.sbuf_tensor("ones_row", [1, 128], F32))
        sm = st.enter_context(nc.sbuf_tensor("sm", [128, 32], F32))
        bnb = st.enter_context(nc.sbuf_tensor("bnb", [128, 24], F32))
        oacc = st.enter_context(nc.sbuf_tensor("oacc", [128, 128], F32))
        epsb = st.enter_context(nc.sbuf_tensor("epsb", [128, 1], F32))

        pex = st.enter_context(nc.psum_tensor("pex", [128, 512], F32))
        pvb = st.enter_context(nc.psum_tensor("pvb", [128, 512], F32))
        ptrA = st.enter_context(nc.psum_tensor("ptrA", [128, 1024], BF16))

        semnames = ["dS", "dG", "dW0", "dW1", "dW2", "dW3", "dX0", "dX1", "dQ0", "dQ1", "dL0", "dL1", "dY0", "dY1", "dA0", "dA1", "tP", "tV", "tA", "tG"]
        handles = {n: st.enter_context(nc.semaphore(n)) for n in semnames}
        block = st.enter_context(nc.Block())
        c = Ctx(block, handles)


        def ph_pro(cc):
            cc.ins("sp", lambda e: e.dma_start(out=idn_f[:], in_=idnf_d.ap()), inc="dS", k=16)
            cc.ins("sp", lambda e: e.dma_start(out=idn_b[:], in_=idnb_d.ap()), inc="dS", k=16)
            cc.ins("dve", lambda e: e.memset(ones_neg[:], -1.0 / D), inc="tV")
            cc.ins("dve", lambda e: e.memset(ones_inv[:], 1.0 / D), inc="tV")
            cc.ins("dve", lambda e: e.memset(ones_row[:], 1.0), inc="tV")
            cc.ins("dve", lambda e: e.memset(epsb[:], EPS), inc="tV")

        c.run_phase(ph_pro)
        c.full_gate()

        with ExitStack() as stAB:
            c_kv = stAB.enter_context(nc.sbuf_tensor("c_kv", [128, KVC * S], BF16))
            kpe_bf = stAB.enter_context(nc.sbuf_tensor("kpe_bf", [64, S], BF16))
            wkvb_bf = stAB.enter_context(nc.sbuf_tensor("wkvb_bf", [128, KVC * H * HKV], BF16))

            with ExitStack() as stA:
                ps4 = stA.enter_context(nc.psum_tensor("psA1", [128, 1024], F32))
                ptrB = stA.enter_context(nc.psum_tensor("ptrB1", [128, 1024], BF16))
                cosk = stA.enter_context(nc.sbuf_tensor("cosk_s", [128, NT * ROPE], F32))
                sink = stA.enter_context(nc.sbuf_tensor("sink_s", [128, NT * ROPE], F32))
                xring = stA.enter_context(nc.sbuf_tensor("xring", [128, 2 * D], F32))
                wqa_bf = stA.enter_context(nc.sbuf_tensor("wqa_bf", [128, DC * QL], BF16))
                wkva_bf = stA.enter_context(nc.sbuf_tensor("wkva_bf", [128, DC * KVOUT], BF16))
                wst = stA.enter_context(nc.sbuf_tensor("wst", [128, 2 * 2048], F32))
                xn1tm = stA.enter_context(nc.sbuf_tensor("xn1tm", [128, 2 * D], BF16))
                xn1fm = stA.enter_context(nc.sbuf_tensor("xn1fm", [128, 2 * D], BF16))
                kst = stA.enter_context(nc.sbuf_tensor("kst", [128, 192], F32))
                qlst = stA.enter_context(nc.sbuf_tensor("qlst", [128, 2 * QL], BF16))

                def ph_a1(cc):
                    pieces = []
                    for dcx in range(DC):
                        pieces.append((wkva_d.ap(), dcx, 0, KVOUT, wkva_bf, dcx * KVOUT))
                    for dcx in range(DC):
                        off = 0
                        while off < QL:
                            w = min(2048, QL - off)
                            pieces.append((wqa_d.ap(), dcx, off, w, wqa_bf, dcx * QL + off))
                            off += w
                    for ccx in range(KVC):
                        off = 0
                        while off < H * HKV:
                            w = min(2048, H * HKV - off)
                            pieces.append((wkvb_d.ap(), ccx, off, w, wkvb_bf, ccx * H * HKV + off))
                            off += w
                    npc = len(pieces)
                    cast_tok = [None] * npc
                    kva_last = [None]
                    qa_last = [None]

                    for j in range(npc):
                        src, chk, off, w, dst, doff = pieces[j]
                        slot = j % 2
                        if j >= 2:
                            cc.waitt("sp", cast_tok[j - 2])
                        dw = "dW0" if slot == 0 else "dW1"
                        cc.ins("sp", lambda e, src=src, chk=chk, off=off, w=w, slot=slot:
                               e.dma_start(out=wst[:, slot * 2048: slot * 2048 + w],
                                           in_=src[chk * 128:(chk + 1) * 128, off:off + w]),
                               inc=dw, k=16)
                        cc.wait("gp", dw, cc.state["sems"][dw])
                        tokc = cc.ins("gp", lambda e, slot=slot, w=w, dst=dst, doff=doff:
                                      e.tensor_copy(out=dst[:, doff:doff + w],
                                                    in_=wst[:, slot * 2048: slot * 2048 + w]),
                                      inc="tG")
                        cast_tok[j] = tokc
                        if j == DC - 1:
                            kva_last[0] = tokc
                        if j == npc - KVC * ((H * HKV + 2047) // 2048) - 1:
                            qa_last[0] = tokc

                    for buf, src in ((cosk, cosk_d), (sink, sink_d)):
                        cc.ins("sp", lambda e, buf=buf, src=src:
                               e.dma_start(out=buf[:], in_=src.ap()), inc="dS", k=16)
                    tabs_tok = cc.tok("dS")

                    xr_free = [None, None]
                    tm_free = [None, None]
                    fm_free = [None, None]
                    ql_free = [None, None]
                    ptr_free = [None, None]
                    bank_free = [None, None]
                    pex_free = [None]

                    for t in range(NT):
                        sl = t % 2
                        dx = "dX0" if sl == 0 else "dX1"
                        cc.waitt("gp", xr_free[sl])
                        cc.ins("gp", lambda e, t=t, sl=sl:
                               e.dma_start(out=xring[:, sl * D:(sl + 1) * D], in_=xa[t]),
                               inc=dx, k=16)
                        cc.wait("dve", dx, cc.state["sems"][dx])
                        nch = (D + 511) // 512
                        for j in range(nch):
                            w = min(512, D - j * 512)
                            cc.ins("dve", lambda e, sl=sl, j=j, w=w:
                                   e.bn_stats(bnb[:, j * 6:(j + 1) * 6],
                                              xring[:, sl * D + j * 512: sl * D + j * 512 + w]),
                                   inc="tV" if j == nch - 1 else None)
                        cc.wait("dve", "tV", cc.state["sems"]["tV"])
                        cc.ins("dve", lambda e, nch=nch: e.bn_aggr(sm[:, 0:2], bnb[:, 0:6 * nch]), inc="tV")
                        agg = cc.tok("tV")
                        cc.waitt("act", agg)
                        cc.ins("act", lambda e: e.activation(out=sm[:, 2:3], in_=sm[:, 1:2],
                                                             func=AF.Ln, bias=epsb[:, 0:1]), inc="tA")
                        cc.wait("act", "tA", cc.state["sems"]["tA"])
                        cc.ins("act", lambda e: e.activation(out=sm[:, 2:3], in_=sm[:, 2:3],
                                                             func=AF.Exp, scale=-0.5), inc="tA")
                        rstd = cc.tok("tA")
                        cc.waitt("dve", rstd)
                        cc.waitt("dve", tm_free[sl])
                        cc.ins("dve", lambda e, sl=sl:
                               e.tensor_scalar(out=xn1tm[:, sl * D:(sl + 1) * D],
                                               in0=xring[:, sl * D:(sl + 1) * D],
                                               scalar1=sm[:, 0:1], scalar2=sm[:, 2:3],
                                               op0=ALU.subtract, op1=ALU.mult),
                               inc="tV")
                        norm = cc.tok("tV")
                        xr_free[sl] = norm
                        cc.waitt("pe", norm)
                        cc.waitt("pe", fm_free[sl])
                        for r in range(DC // 4):
                            half = r % 2
                            ptr_t = (ptrA, ptrB)[half]
                            cc.waitt("pe", ptr_free[half])
                            for j in range(4):
                                dcx = r * 4 + j
                                cc.ins("pe", lambda e, sl=sl, dcx=dcx, j=j, ptr_t=ptr_t:
                                       e.transpose(ptr_t[:, j * 128:(j + 1) * 128],
                                                   xn1tm[:, sl * D + dcx * 128: sl * D + (dcx + 1) * 128],
                                                   idn_b[:]),
                                       inc="tP" if j == 3 else None)
                            tr = cc.tok("tP")
                            cc.waitt("dve", tr)
                            cc.ins("dve", lambda e, sl=sl, r=r, ptr_t=ptr_t:
                                   e.tensor_copy(out=xn1fm[:, sl * D + r * 512: sl * D + (r + 1) * 512],
                                                 in_=ptr_t[:, 0:512]),
                                   inc="tV")
                            ptr_free[half] = cc.tok("tV")
                        tm_free[sl] = cc.tok("tP")
                        fm_ready = cc.tok("tV")

                        cc.waitt("pe", fm_ready)
                        cc.waitt("pe", kva_last[0])
                        oc_list = [("kva", o) for o in range(KVC + 1)]
                        if t % 2 == 0:
                            oc_list += [("qa", o) for o in range(QLC)]
                        for idx, (kind, oc) in enumerate(oc_list):
                            if kind == "qa":
                                cc.waitt("pe", qa_last[0])
                            bk = idx % 2
                            bko = bk * 512
                            cc.waitt("pe", bank_free[bk])
                            is_kpe = (kind == "kva" and oc == KVC)
                            m = 64 if is_kpe else 128
                            for dcx in range(DC):
                                if kind == "kva":
                                    lo = dcx * KVOUT + oc * 128
                                    lhs = wkva_bf
                                else:
                                    lo = dcx * QL + oc * 128
                                    lhs = wqa_bf
                                cc.ins("pe", lambda e, bko=bko, lhs=lhs, lo=lo, m=m, sl=sl, dcx=dcx,
                                       first=(dcx == 0), last=(dcx == DC - 1):
                                       e.matmul(ps4[0:m, bko:bko + 128],
                                                lhs[:, lo:lo + m],
                                                xn1fm[:, sl * D + dcx * 128: sl * D + (dcx + 1) * 128],
                                                start=first, stop=last),
                                       inc="tP" if dcx == DC - 1 else None)
                            grp = cc.tok("tP")
                            if kind == "kva" and oc < KVC:
                                cc.waitt("act", grp)
                                cc.ins("act", lambda e, bko=bko, oc=oc, t=t:
                                       e.activation(out=c_kv[:, oc * S + t * 128: oc * S + (t + 1) * 128],
                                                    in_=ps4[:, bko:bko + 128], func=AF.Copy),
                                       inc="tA")
                                bank_free[bk] = cc.tok("tA")
                            elif is_kpe:
                                cc.waitt("act", grp)
                                cc.ins("act", lambda e, bko=bko:
                                       e.activation(out=kst[0:64, 0:128],
                                                    in_=ps4[0:64, bko:bko + 128], func=AF.Copy),
                                       inc="tA")
                                kc = cc.tok("tA")
                                bank_free[bk] = kc
                                cc.waitt("pe", kc)
                                cc.waitt("pe", pex_free[0])
                                cc.ins("pe", lambda e: e.transpose(pex[:, 0:64], kst[0:64, 0:128], idn_f[0:64, 0:64]),
                                       inc="tP")
                                ktr = cc.tok("tP")
                                cc.waitt("dve", ktr)
                                cc.waitt("dve", tabs_tok)
                                cc.ins("dve", lambda e, t=t:
                                       e.tensor_tensor(out=kst[:, 64:128], in0=pex[:, 0:64],
                                                       in1=cosk[:, t * ROPE:(t + 1) * ROPE], op=ALU.mult),
                                       inc="tV")
                                cc.ins("dve", lambda e, t=t:
                                       e.tensor_tensor(out=kst[:, 128:192],
                                                       in0=swap_view(pex[:, 0:64], 64),
                                                       in1=sink[:, t * ROPE:(t + 1) * ROPE], op=ALU.mult),
                                       inc="tV")
                                cc.wait("dve", "tV", cc.state["sems"]["tV"])
                                cc.ins("dve", lambda e:
                                       e.tensor_tensor(out=kst[:, 64:128], in0=kst[:, 64:128],
                                                       in1=kst[:, 128:192], op=ALU.add),
                                       inc="tV")
                                krp = cc.tok("tV")
                                cc.waitt("pe", krp)
                                cc.ins("pe", lambda e: e.transpose(pex[0:64, 128:256], kst[:, 64:128], idn_f[:]),
                                       inc="tP")
                                ktr2 = cc.tok("tP")
                                cc.waitt("act", ktr2)
                                cc.ins("act", lambda e, t=t:
                                       e.activation(out=kpe_bf[0:64, t * 128:(t + 1) * 128],
                                                    in_=pex[0:64, 128:256], func=AF.Copy),
                                       inc="tA")
                                pex_free[0] = cc.tok("tA")
                            else:
                                qsl = (t // 2) % 2
                                cc.waitt("dve", grp)
                                if oc == 0:
                                    cc.waitt("dve", ql_free[qsl])
                                cc.ins("dve", lambda e, bko=bko, oc=oc, qsl=qsl:
                                       e.tensor_copy(out=qlst[:, qsl * QL + oc * 128: qsl * QL + (oc + 1) * 128],
                                                     in_=ps4[:, bko:bko + 128]),
                                       inc="tV")
                                bank_free[bk] = cc.tok("tV")
                                if oc == QLC - 1:
                                    qev = cc.tok("tV")
                                    cc.waitt("sp", qev)
                                    dl = "dL0" if qsl == 0 else "dL1"
                                    cc.ins("sp", lambda e, t=t, qsl=qsl:
                                           e.dma_start(out=qlat_a[t // 2],
                                                       in_=qlst[:, qsl * QL:(qsl + 1) * QL]),
                                           inc=dl, k=16)
                                    ql_free[qsl] = cc.tok(dl)
                        fm_free[sl] = cc.tok("tP")

                c.run_phase(ph_a1)
            c.full_gate()

            with ExitStack() as stA2:
                ps4 = stA2.enter_context(nc.psum_tensor("psA2", [128, 2048], F32))
                cosq = stA2.enter_context(nc.sbuf_tensor("cosq2", [128, NO * ROPE], F32))
                sinq = stA2.enter_context(nc.sbuf_tensor("sinq2", [128, NO * ROPE], F32))
                wqb_bf = stA2.enter_context(nc.sbuf_tensor("wqb_bf", [128, QLC * H * HD], BF16))
                wst2 = stA2.enter_context(nc.sbuf_tensor("wst2", [128, 2 * 2048], F32))
                qlring = stA2.enter_context(nc.sbuf_tensor("qlring", [128, 2 * QL], BF16))
                qstage = stA2.enter_context(nc.sbuf_tensor("qstage", [128, 2 * H * HD], BF16))

                def ph_a2(cc):
                    for buf, src in ((cosq, cosq_d), (sinq, sinq_d)):
                        cc.ins("sp", lambda e, buf=buf, src=src:
                               e.dma_start(out=buf[:], in_=src.ap()), inc="dS", k=16)
                    tabs = cc.tok("dS")
                    pieces = []
                    for lcx in range(QLC):
                        off = 0
                        while off < H * HD:
                            w = min(2048, H * HD - off)
                            pieces.append((lcx, off, w))
                            off += w
                    cast_tok = [None] * len(pieces)
                    wqb_last = [None]
                    pi = [0]

                    def emit_pieces(n):
                        j = pi[0]
                        for _ in range(n):
                            if j >= len(pieces):
                                break
                            lcx, off, w = pieces[j]
                            slot = j % 2
                            if j >= 2:
                                cc.waitt("sp", cast_tok[j - 2])
                            dw = "dW0" if slot == 0 else "dW1"
                            cc.ins("sp", lambda e, lcx=lcx, off=off, w=w, slot=slot:
                                   e.dma_start(out=wst2[:, slot * 2048: slot * 2048 + w],
                                               in_=wqb_d.ap()[lcx * 128:(lcx + 1) * 128, off:off + w]),
                                   inc=dw, k=16)
                            cc.wait("gp", dw, cc.state["sems"][dw])
                            tokc = cc.ins("gp", lambda e, slot=slot, w=w, lcx=lcx, off=off:
                                          e.tensor_copy(out=wqb_bf[:, lcx * H * HD + off: lcx * H * HD + off + w],
                                                        in_=wst2[:, slot * 2048: slot * 2048 + w]),
                                          inc="tG")
                            cast_tok[j] = tokc
                            if j == len(pieces) - 1:
                                wqb_last[0] = tokc
                            j += 1
                        pi[0] = j

                    emit_pieces(len(pieces))

                    qr_free = [None, None]
                    qs_free = [None, None]
                    nslab = (H * HD + 511) // 512
                    bank_free = [None] * 4

                    for i in range(NO):
                        sl = i % 2
                        dq = "dQ0" if sl == 0 else "dQ1"
                        cc.waitt("gp", qr_free[sl])
                        cc.ins("gp", lambda e, i=i, sl=sl:
                               e.dma_start(out=qlring[:, sl * QL:(sl + 1) * QL], in_=qlat_a[i]),
                               inc=dq, k=16)
                        qload = cc.tok(dq)
                        cc.waitt("pe", qload)
                        cc.waitt("pe", wqb_last[0])
                        if i == 0:
                            pass
                        for s in range(nslab):
                            w = min(512, H * HD - s * 512)
                            bk = s % 4
                            cc.waitt("pe", bank_free[bk])
                            for lcx in range(QLC):
                                cc.ins("pe", lambda e, bk=bk, w=w, lcx=lcx, s=s, sl=sl,
                                       first=(lcx == 0), last=(lcx == QLC - 1):
                                       e.matmul(ps4[:, bk * 512: bk * 512 + w],
                                                qlring[:, sl * QL + lcx * 128: sl * QL + (lcx + 1) * 128],
                                                wqb_bf[:, lcx * H * HD + s * 512: lcx * H * HD + s * 512 + w],
                                                start=first, stop=last),
                                       inc="tP" if lcx == QLC - 1 else None)
                            grp = cc.tok("tP")
                            cc.waitt("act", grp)
                            if s == 0:
                                cc.waitt("act", qs_free[sl])
                            cc.ins("act", lambda e, bk=bk, w=w, s=s, sl=sl:
                                   e.activation(out=qstage[:, sl * H * HD + s * 512: sl * H * HD + s * 512 + w],
                                                in_=ps4[:, bk * 512: bk * 512 + w], func=AF.Copy),
                                   inc="tA")
                            ev = cc.tok("tA")
                            cc.waitt("dve", grp)
                            cc.waitt("dve", ev)
                            cc.wait("dve", "dS", tabs[1])
                            for hh in range(w // HD):
                                pecol = bk * 512 + hh * HD + NOPE
                                qcol = sl * H * HD + s * 512 + hh * HD + NOPE
                                cc.wait("dve", "tV", cc.state["sems"]["tV"])
                                cc.ins("dve", lambda e, pecol=pecol, i=i:
                                       e.tensor_tensor(out=kst_a2[:, 0:64],
                                                       in0=ps4[:, pecol:pecol + 64],
                                                       in1=cosq[:, i * ROPE:(i + 1) * ROPE], op=ALU.mult),
                                       inc="tV")
                                cc.ins("dve", lambda e, pecol=pecol, i=i:
                                       e.tensor_tensor(out=kst_a2[:, 64:128],
                                                       in0=swap_view(ps4[:, pecol:pecol + 64], 64),
                                                       in1=sinq[:, i * ROPE:(i + 1) * ROPE], op=ALU.mult),
                                       inc="tV")
                                cc.wait("dve", "tV", cc.state["sems"]["tV"])
                                cc.ins("dve", lambda e, qcol=qcol:
                                       e.tensor_tensor(out=qstage[:, qcol:qcol + 64],
                                                       in0=kst_a2[:, 0:64], in1=kst_a2[:, 64:128],
                                                       op=ALU.add),
                                       inc="tV")
                            bank_free[bk] = cc.tok("tV")
                        qr_free[sl] = cc.tok("tP")
                        done = cc.tok("tV")
                        cc.waitt("sp", done)
                        cc.wait("sp", "tA", cc.state["sems"]["tA"])
                        dl = "dL0" if sl == 0 else "dL1"
                        cc.ins("sp", lambda e, i=i, sl=sl:
                               e.dma_start(out=q_a[i], in_=qstage[:, sl * H * HD:(sl + 1) * H * HD]),
                               inc=dl, k=16)
                        qs_free[sl] = cc.tok(dl)

                kst_a2 = stA2.enter_context(nc.sbuf_tensor("kst_a2", [128, 128], F32))
                c.run_phase(ph_a2)
            c.full_gate()

            with ExitStack() as stB:
                psB = stB.enter_context(nc.psum_tensor("psB", [128, 1536], F32))
                stgB = stB.enter_context(nc.psum_tensor("stgB", [128, 1024], BF16))
                pacc2 = stB.enter_context(nc.psum_tensor("pacc2", [128, 512], F32))
                masks_sb = stB.enter_context(nc.sbuf_tensor("masks_sb", [128, NO * 512], BF16))
                kfull = stB.enter_context(nc.sbuf_tensor("kfull", [128, 2 * S], BF16))
                v_tm = stB.enter_context(nc.sbuf_tensor("v_tm", [128, 2 * S], BF16))
                qh = stB.enter_context(nc.sbuf_tensor("qh", [128, 2 * NO * 128], BF16))
                qTb = stB.enter_context(nc.sbuf_tensor("qTb", [128, 2 * 128], BF16))
                qsq_sb = stB.enter_context(nc.sbuf_tensor("qsq_sb", [128, 128], BF16))
                kamb = stB.enter_context(nc.sbuf_tensor("kamb", [128, 18], BF16))
                kmbc = stB.enter_context(nc.sbuf_tensor("kmbc", [128, 2 * 128], BF16))
                ksq_sb = stB.enter_context(nc.sbuf_tensor("ksq_sb", [128, 512], BF16))
                kb2 = stB.enter_context(nc.sbuf_tensor("kb2", [128, 2], F32))
                bnd = stB.enter_context(nc.sbuf_tensor("bnd", [128, 2 * NO], F32))
                Pbuf = stB.enter_context(nc.sbuf_tensor("Pbuf", [128, 4 * 1024], BF16))
                PTs = stB.enter_context(nc.sbuf_tensor("PTs", [128, 2 * 1024], BF16))
                obf = stB.enter_context(nc.sbuf_tensor("obf", [128, 4 * 128], BF16))
                astage = stB.enter_context(nc.sbuf_tensor("astage", [128, 2 * TOK], BF16))

                NKS = S // 512
                ACCB = (pvb, pacc2)
                STG = (ptrA, stgB)
                PSOFF = (0, 1024)
                PSCAP = (1024, 512)

                def ph_b(cc):
                    cc.ins("sp", lambda e: e.dma_start(out=masks_sb[:], in_=masks_d.ap()), inc="dS", k=16)
                    masks_tok = cc.tok("dS")
                    for nb in range(2):
                        cc.ins("dve", lambda e, nb=nb:
                               e.tensor_copy(out=kfull[64:128, nb * S:(nb + 1) * S], in_=kpe_bf[:]),
                               inc="tV")
                    kpe_tok = cc.tok("tV")

                    qh_toks = [None, None]
                    qh_free = [[], []]
                    ast_free = [None, None]
                    pe_end = [None, None]
                    pkv_free = [None]
                    kv_done_k = [None, None]
                    kv_done_v = [None, None]
                    kam_done = [None, None]
                    kmbc_tok = [None, None]
                    kb2_tok = [None, None]
                    km_dma_guard = [None, None]
                    bnd_tok = [None, None]
                    bnd_guard = [None, None]
                    e_prev = [None, None]
                    pb_prev = [None, None, None, None]
                    l_guard = [[] for _ in range(6)]
                    lrun_guard = [None, None]
                    rec_guard = [None, None]
                    rec_rot = [0]
                    obf_free = [None, None, None, None]
                    qT_free = [None, None]
                    qT_tok = [None, None]
                    ptrP_war = [None, None]
                    ptrP_rot = [0]
                    av_prev = [None, None]
                    pacc_free = [None, None]
                    pending_o = []
                    tac_seq = [0]

                    def qh_load(h):
                        if h >= H:
                            return
                        hsl = h % 2
                        dq = "dQ0" if hsl == 0 else "dQ1"
                        for t in qh_free[hsl]:
                            cc.waitt("gp", t)
                        qsrc = bass.AP(q_sc, h * HD,
                                       [[H * HD, 128], [128 * H * HD, NO], [1, 128]])
                        cc.ins("gp", lambda e, hsl=hsl, qsrc=qsrc:
                               e.dma_start(out=bass.AP(qh, hsl * NO * 128,
                                                       [[2 * NO * 128, 128], [128, NO], [1, 128]]),
                                           in_=qsrc),
                               inc=dq, k=16)
                        qh_toks[hsl] = cc.tok(dq)

                    unit_tok = {}

                    def emit_unit(h1, u, phase=None):
                        nb = h1 % 2
                        ks = u // 2
                        if u % 2 == 0:
                            if phase == 0:
                                cc.waitt("pe", pkv_free[0])
                                for ccx in range(KVC):
                                    lo = ccx * H * HKV + h1 * HKV
                                    cc.ins("pe", lambda e, ccx=ccx, lo=lo, ks=ks:
                                           e.matmul(pex[0:64, 0:512],
                                                    wkvb_bf[:, lo:lo + 64],
                                                    c_kv[:, ccx * S + ks * 512: ccx * S + (ks + 1) * 512],
                                                    start=(ccx == 0), stop=(ccx == KVC - 1)),
                                           inc="tP" if ccx == KVC - 1 else None)
                                unit_tok[(h1, u)] = cc.tok("tP")
                                return
                            kt = unit_tok[(h1, u)]
                            cc.waitt("act", kt)
                            cc.waitt("act", pe_end[nb])
                            cc.ins("act", lambda e, nb=nb, ks=ks:
                                   e.activation(out=kfull[0:64, nb * S + ks * 512: nb * S + (ks + 1) * 512],
                                                in_=pex[0:64, 0:512], func=AF.Copy, scale=SCALE),
                                   inc="tA")
                            tokk = cc.tok("tA")
                            pkv_free[0] = tokk
                            kv_done_k[nb] = tokk
                            cc.waitt("dve", tokk)
                            cc.waitt("dve", kpe_tok)
                            cc.ins("dve", lambda e, nb=nb, ks=ks:
                                   e.tensor_tensor(out=ksq_sb[:],
                                                   in0=kfull[:, nb * S + ks * 512: nb * S + (ks + 1) * 512],
                                                   in1=kfull[:, nb * S + ks * 512: nb * S + (ks + 1) * 512],
                                                   op=ALU.mult),
                                   inc="tV")
                            cc.wait("dve", "tV", cc.state["sems"]["tV"])
                            cc.ins("dve", lambda e, nb=nb, ks=ks:
                                   e.tensor_reduce(kamb[:, nb * 8 + ks: nb * 8 + ks + 1],
                                                   ksq_sb[:], axis=AX.X, op=ALU.max),
                                   inc="tV")
                            if ks == NKS - 1:
                                cc.wait("dve", "tV", cc.state["sems"]["tV"])
                                cc.waitt("dve", km_dma_guard[nb])
                                cc.ins("dve", lambda e, nb=nb:
                                       e.tensor_reduce(kamb[:, 16 + nb: 17 + nb],
                                                       kamb[:, nb * 8: nb * 8 + 8],
                                                       axis=AX.X, op=ALU.max),
                                       inc="tV")
                                kam_done[nb] = cc.tok("tV")
                                cc.waitt("sp", kam_done[nb])
                                cc.ins("sp", lambda e, nb=nb:
                                       e.dma_start(out=bass.AP(kmsc, nb * 128, [[1, 128], [1, 1]]),
                                                   in_=kamb[:, 16 + nb: 17 + nb]),
                                       inc="dS", k=16)
                                t1 = cc.tok("dS")
                                km_dma_guard[nb] = t1
                                cc.waitt("sp", t1)
                                cc.waitt("sp", kb2_tok[nb])
                                cc.ins("sp", lambda e, nb=nb:
                                       e.dma_start(out=kmbc[:, nb * 128:(nb + 1) * 128],
                                                   in_=bass.AP(kmsc, nb * 128, [[0, 128], [1, 128]])),
                                       inc="dS", k=16)
                                kmbc_tok[nb] = cc.tok("dS")
                                cc.waitt("dve", kmbc_tok[nb])
                                cc.ins("dve", lambda e, nb=nb:
                                       e.tensor_reduce(kb2[:, nb:nb + 1],
                                                       kmbc[:, nb * 128:(nb + 1) * 128],
                                                       axis=AX.X, op=ALU.add),
                                       inc="tV")
                                kb2_tok[nb] = cc.tok("tV")
                        else:
                            if phase == 0:
                                cc.waitt("pe", pkv_free[0])
                                for kc in range(4):
                                    for ccx in range(KVC):
                                        lo = ccx * H * HKV + h1 * HKV + NOPE
                                        cc.ins("pe", lambda e, kc=kc, ccx=ccx, lo=lo, ks=ks:
                                               e.matmul(pex[:, kc * 128:(kc + 1) * 128],
                                                        c_kv[:, ccx * S + ks * 512 + kc * 128: ccx * S + ks * 512 + (kc + 1) * 128],
                                                        wkvb_bf[:, lo:lo + VD],
                                                        start=(ccx == 0), stop=(ccx == KVC - 1)),
                                               inc="tP" if (kc == 3 and ccx == KVC - 1) else None)
                                unit_tok[(h1, u)] = cc.tok("tP")
                                return
                            vt = unit_tok[(h1, u)]
                            cc.waitt("dve", vt)
                            cc.waitt("dve", pe_end[nb])
                            cc.ins("dve", lambda e, nb=nb, ks=ks:
                                   e.tensor_copy(out=v_tm[:, nb * S + ks * 512: nb * S + (ks + 1) * 512],
                                                 in_=pex[:, 0:512]),
                                   inc="tV")
                            tokv = cc.tok("tV")
                            pkv_free[0] = tokv
                            kv_done_v[nb] = tokv

                    def emit_bounds(h):
                        hsl = h % 2
                        hb = h % 2
                        cc.waitt("dve", qh_toks[hsl])
                        cc.waitt("dve", kb2_tok[hb])
                        cc.waitt("dve", bnd_guard[hsl])
                        for i in range(NO):
                            cc.wait("dve", "tV", cc.state["sems"]["tV"])
                            cc.ins("dve", lambda e, i=i:
                                   e.tensor_tensor(out=qsq_sb[:],
                                                   in0=qh[:, hsl * NO * 128 + i * 128: hsl * NO * 128 + (i + 1) * 128],
                                                   in1=qh[:, hsl * NO * 128 + i * 128: hsl * NO * 128 + (i + 1) * 128],
                                                   op=ALU.mult),
                                   inc="tV")
                            cc.wait("dve", "tV", cc.state["sems"]["tV"])
                            cc.ins("dve", lambda e, i=i:
                                   e.tensor_reduce(bnd[:, hsl * NO + i: hsl * NO + i + 1],
                                                   qsq_sb[:], axis=AX.X, op=ALU.add),
                                   inc="tV")
                        cc.wait("dve", "tV", cc.state["sems"]["tV"])
                        cc.ins("dve", lambda e:
                               e.tensor_scalar(out=bnd[:, hsl * NO: hsl * NO + NO],
                                               in0=bnd[:, hsl * NO: hsl * NO + NO],
                                               scalar1=kb2[:, hb:hb + 1], scalar2=None,
                                               op0=ALU.mult),
                               inc="tV")
                        vtok = cc.tok("tV")
                        cc.waitt("act", vtok)
                        cc.waitt("act", bnd_guard[hsl])
                        cc.ins("act", lambda e:
                               e.activation(out=bnd[:, hsl * NO: hsl * NO + NO],
                                            in_=bnd[:, hsl * NO: hsl * NO + NO],
                                            func=AF.Ln),
                               inc="tA")
                        cc.wait("act", "tA", cc.state["sems"]["tA"])
                        cc.ins("act", lambda e:
                               e.activation(out=bnd[:, hsl * NO: hsl * NO + NO],
                                            in_=bnd[:, hsl * NO: hsl * NO + NO],
                                            func=AF.Exp, scale=0.5),
                               inc="tA")
                        atok = cc.tok("tA")
                        cc.waitt("dve", atok)
                        cc.ins("dve", lambda e:
                               e.tensor_scalar(out=bnd[:, hsl * NO: hsl * NO + NO],
                                               in0=bnd[:, hsl * NO: hsl * NO + NO],
                                               scalar1=-1.05, scalar2=-1.0,
                                               op0=ALU.mult, op1=ALU.add),
                               inc="tV")
                        bnd_tok[hsl] = cc.tok("tV")
                        qh_free[hsl].append(bnd_tok[hsl])

                    def flush_o(all_=True):
                        while pending_o:
                            if not all_ and pending_o[0][4] > tac_seq[0] - 4:
                                break
                            hsl2, ob2, i2, o_tok2, _sq = pending_o.pop(0)
                            half = ptrP_rot[0] % 2
                            ptrP_rot[0] += 1
                            stg = STG[half]
                            cc.waitt("pe", o_tok2)
                            cc.waitt("pe", ptrP_war[half])
                            cc.ins("pe", lambda e, ob2=ob2, stg=stg:
                                   e.transpose(stg[:, 0:128],
                                               obf[:, ob2 * 128:(ob2 + 1) * 128],
                                               idn_b[:]), inc="tP")
                            otr = cc.tok("tP")
                            obf_free[ob2] = otr
                            cc.waitt("dve", otr)
                            cc.waitt("dve", ast_free[hsl2])
                            cc.ins("dve", lambda e, i2=i2, stg=stg, hsl2=hsl2:
                                   e.tensor_copy(out=astage[:, hsl2 * TOK + i2 * 128: hsl2 * TOK + (i2 + 1) * 128],
                                                 in_=stg[:, 0:128]),
                                   inc="tV")
                            ptrP_war[half] = cc.tok("tV")

                    def emit_qT(h, i):
                        hsl = h % 2
                        par = i % 2
                        half = ptrP_rot[0] % 2
                        ptrP_rot[0] += 1
                        stg = STG[half]
                        cc.waitt("pe", qh_toks[hsl])
                        cc.waitt("pe", ptrP_war[half])
                        cc.ins("pe", lambda e, i=i, stg=stg, hsl=hsl:
                               e.transpose(stg[:, 0:128],
                                           qh[:, hsl * NO * 128 + i * 128: hsl * NO * 128 + (i + 1) * 128],
                                           idn_b[:]), inc="tP")
                        tr = cc.tok("tP")
                        qh_free[hsl].append(tr)
                        cc.waitt("dve", tr)
                        cc.waitt("dve", qT_free[par])
                        cc.ins("dve", lambda e, stg=stg, par=par:
                               e.tensor_copy(out=qTb[:, par * 128:(par + 1) * 128],
                                             in_=stg[:, 0:128]), inc="tV")
                        qT_tok[par] = cc.tok("tV")
                        ptrP_war[half] = qT_tok[par]

                    def emit_S(jb):
                        i, s2, sk0, L = jb["i"], jb["s"], jb["sk0"], jb["L"]
                        h, hb, hsl = jb["h"], jb["hb"], jb["hsl"]
                        jp = jb["jp"]
                        po = PSOFF[jp]
                        par = i % 2
                        jb["par"] = par
                        cc.waitt("pe", e_prev[jp])
                        cc.waitt("pe", kv_done_k[hb])
                        cc.waitt("pe", kpe_tok)
                        cc.waitt("pe", qT_tok[par])
                        for k0 in range(0, L, 512):
                            w = min(512, L - k0)
                            cc.ins("pe", lambda e, po=po, k0=k0, w=w, sk0=sk0, par=par, hb=hb:
                                   e.matmul(psB[:, po + k0: po + k0 + w],
                                            qTb[:, par * 128:(par + 1) * 128],
                                            kfull[:, hb * S + sk0 + k0: hb * S + sk0 + k0 + w],
                                            start=True, stop=True),
                                   inc="tP" if k0 + w >= L else None)
                        sc_tok = cc.tok("tP")
                        qT_free[par] = sc_tok
                        exp_dep = sc_tok
                        if s2 == jb["nseg"] - 1:
                            wm = 512 if jb["E"] % 512 == 0 else jb["E"] % 512
                            cc.waitt("dve", sc_tok)
                            cc.waitt("dve", masks_tok)
                            cc.ins("dve", lambda e, po=po, L=L, wm=wm, i=i:
                                   e.tensor_tensor(out=psB[:, po + L - wm: po + L],
                                                   in0=psB[:, po + L - wm: po + L],
                                                   in1=masks_sb[:, i * 512 + 512 - wm:(i + 1) * 512],
                                                   op=ALU.add),
                                   inc="tV")
                            exp_dep = cc.tok("tV")
                        lcol = jb["gidx"] % 6
                        pb = jb["gidx"] % 4
                        cc.waitt("act", exp_dep)
                        cc.waitt("act", bnd_tok[hsl])
                        cc.waitt("act", pb_prev[pb])
                        for t in l_guard[jb["gidx"] % 6]:
                            cc.waitt("act", t)
                        l_guard[jb["gidx"] % 6] = []
                        cc.ins("act", lambda e, pb=pb, po=po, L=L, lcol=lcol, hsl=hsl, i=i:
                               e.activation(out=Pbuf[:, pb * 1024: pb * 1024 + L],
                                            in_=psB[:, po: po + L],
                                            func=AF.Exp, bias=bnd[:, hsl * NO + i: hsl * NO + i + 1],
                                            accum_out=sm[:, lcol:lcol + 1]),
                               inc="tA")
                        jb["e_tok"] = cc.tok("tA")
                        e_prev[jp] = jb["e_tok"]
                        bnd_guard[hsl] = jb["e_tok"]
                        jb["lcol"] = lcol
                        jb["pb"] = pb

                    def emit_T(jb):
                        L = jb["L"]
                        jp, pb = jb["jp"], jb["pb"]
                        tac_seq[0] += 1
                        flush_o(all_=False)
                        nch = L // 128
                        ngrp = (nch + 3) // 4
                        cc.waitt("pe", jb["e_tok"])
                        copy_toks = []
                        for g in range(ngrp):
                            gw = min(512, L - g * 512)
                            half = ptrP_rot[0] % 2
                            ptrP_rot[0] += 1
                            stg = STG[half]
                            cc.waitt("pe", ptrP_war[half])
                            for cch in range(gw // 128):
                                cix = g * 4 + cch
                                cc.ins("pe", lambda e, stg=stg, cch=cch, cix=cix, pb=pb:
                                       e.transpose(stg[:, cch * 128:(cch + 1) * 128],
                                                   Pbuf[:, pb * 1024 + cix * 128: pb * 1024 + (cix + 1) * 128],
                                                   idn_b[:]),
                                       inc="tP" if cch == gw // 128 - 1 else None)
                            gt = cc.tok("tP")
                            cc.waitt("dve", gt)
                            if g == 0:
                                cc.waitt("dve", av_prev[jp])
                            cc.ins("dve", lambda e, stg=stg, g=g, gw=gw, jp=jp:
                                   e.tensor_copy(out=PTs[:, jp * 1024 + g * 512: jp * 1024 + g * 512 + gw],
                                                 in_=stg[:, 0:gw]),
                                   inc="tV")
                            ct = cc.tok("tV")
                            copy_toks.append(ct)
                            ptrP_war[half] = ct
                        jb["copy_toks"] = copy_toks
                        jb["T_last_pe"] = cc.tok("tP")
                        pb_prev[pb] = jb["T_last_pe"]

                    def emit_AC(jb):
                        i, s2, sk0, L, nseg = jb["i"], jb["s"], jb["sk0"], jb["L"], jb["nseg"]
                        h, hb, hsl = jb["h"], jb["hb"], jb["hsl"]
                        jp, par, pb = jb["jp"], jb["par"], jb["pb"]
                        nch = L // 128
                        ngrp = (nch + 3) // 4
                        copy_toks = jb["copy_toks"]
                        if s2 == 0:
                            cc.waitt("pe", pacc_free[par])
                        cc.waitt("pe", kv_done_v[hb])
                        for g in range(ngrp):
                            gw = min(512, L - g * 512)
                            cc.waitt("pe", copy_toks[g])
                            for cch in range(gw // 128):
                                kt = (sk0 + g * 512 + cch * 128) // 128
                                fst = (s2 == 0 and g == 0 and cch == 0)
                                lst = (s2 == nseg - 1 and g == ngrp - 1 and cch == gw // 128 - 1)
                                cc.ins("pe", lambda e, par=par, g=g, cch=cch, kt=kt, jp=jp, hb=hb,
                                       fst=fst, lst=lst:
                                       e.matmul(ACCB[par][:, 0:128],
                                                PTs[:, jp * 1024 + g * 512 + cch * 128: jp * 1024 + g * 512 + (cch + 1) * 128],
                                                v_tm[:, hb * S + kt * 128: hb * S + (kt + 1) * 128],
                                                start=fst, stop=lst),
                                       inc="tP" if (g == ngrp - 1 and cch == gw // 128 - 1) else None)
                        av = cc.tok("tP")
                        av_prev[jp] = av
                        lcol = jb["lcol"]
                        if nseg > 1 and s2 >= 1:
                            prev = jb["prev"]
                            cc.waitt("gp", jb["e_tok"])
                            cc.wait("gp", "tG", cc.state["sems"]["tG"])
                            if s2 == 1:
                                cc.waitt("gp", prev["e_tok"])
                                cc.waitt("gp", lrun_guard[par])
                                cc.ins("gp", lambda e, par=par, lcol=lcol, plcol=prev["lcol"]:
                                       e.tensor_tensor(out=sm[:, 8 + par:9 + par],
                                                       in0=sm[:, plcol:plcol + 1],
                                                       in1=sm[:, lcol:lcol + 1], op=ALU.add),
                                       inc="tG")
                            else:
                                cc.ins("gp", lambda e, par=par, lcol=lcol:
                                       e.tensor_tensor(out=sm[:, 8 + par:9 + par],
                                                       in0=sm[:, 8 + par:9 + par],
                                                       in1=sm[:, lcol:lcol + 1], op=ALU.add),
                                       inc="tG")
                            l_tok = cc.tok("tG")
                            l_guard[jb["gidx"] % 6].append(l_tok)
                            if s2 == 1:
                                l_guard[prev["gidx"] % 6].append(l_tok)
                        if s2 == nseg - 1:
                            rr = rec_rot[0] % 2
                            rec_rot[0] += 1
                            rec_col = 18 + rr
                            if nseg == 1:
                                cc.waitt("dve", jb["e_tok"])
                                lsrc = lcol
                            else:
                                cc.waitt("dve", l_tok)
                                lsrc = 8 + par
                            cc.waitt("dve", rec_guard[rr])
                            cc.ins("dve", lambda e, rec_col=rec_col, lsrc=lsrc:
                                   e.reciprocal(sm[:, rec_col:rec_col + 1], sm[:, lsrc:lsrc + 1]),
                                   inc="tV")
                            rec = cc.tok("tV")
                            if nseg == 1:
                                l_guard[jb["gidx"] % 6].append(rec)
                            else:
                                lrun_guard[par] = rec
                            ob4 = i % 4
                            cc.waitt("dve", av)
                            cc.waitt("dve", obf_free[ob4])
                            cc.wait("dve", "tV", cc.state["sems"]["tV"])
                            cc.ins("dve", lambda e, ob4=ob4, par=par, rec_col=rec_col:
                                   e.tensor_scalar(out=obf[:, ob4 * 128:(ob4 + 1) * 128],
                                                   in0=ACCB[par][:, 0:128],
                                                   scalar1=sm[:, rec_col:rec_col + 1], scalar2=None,
                                                   op0=ALU.mult),
                                   inc="tV")
                            o_tok = cc.tok("tV")
                            rec_guard[rr] = o_tok
                            pacc_free[par] = o_tok
                            pending_o.append((hsl, ob4, i, o_tok, tac_seq[0]))
                        if jb.get("last_of_head"):
                            flush_o()
                            pe_end[hb] = cc.tok("tP")
                            qh_free[hsl].append(pe_end[hb])
                            fin = cc.tok("tV")
                            cc.waitt("gp", fin)
                            da = "dA0" if hsl == 0 else "dA1"
                            cc.ins("gp", lambda e, h=h, hsl=hsl:
                                   e.dma_start(out=attn_a[h], in_=astage[:, hsl * TOK:(hsl + 1) * TOK]),
                                   inc=da, k=16)
                            ast_free[hsl] = cc.tok(da)

                    gjobs = []
                    head_start = {}
                    fill_map = {}
                    prep_map = {}
                    bmap = {}
                    for h in range(H):
                        head_start[len(gjobs)] = h
                        base = len(gjobs)
                        first_job = {}
                        for i in range(NO):
                            E = 256 * (i + 1)
                            first_job[i] = len(gjobs)
                            sk0 = 0
                            tj = []
                            for_prev = None
                            while sk0 < E:
                                jp = len(gjobs) % 2
                                L = min(PSCAP[jp], E - sk0)
                                jb = dict(i=i, s=len(tj), sk0=sk0, L=L, E=E, jp=jp,
                                          h=h, hb=h % 2, hsl=h % 2, gidx=len(gjobs),
                                          prev=for_prev)
                                gjobs.append(jb)
                                tj.append(jb)
                                for_prev = jb
                                sk0 += L
                            for jb2 in tj:
                                jb2["nseg"] = len(tj)
                        gjobs[-1]["last_of_head"] = True
                        nj = len(gjobs) - base
                        if h + 1 < H:
                            uorder = list(range(0, 2 * NKS, 2)) + list(range(1, 2 * NKS, 2))
                            for slot, u in enumerate(uorder):
                                pos = base + 2 + (slot * (nj - 3)) // (2 * NKS)
                                fill_map.setdefault(pos, []).append((h + 1, u, 0))
                                fill_map.setdefault(pos + 1, []).append((h + 1, u, 1))
                        for i in range(1, NO):
                            prep_map.setdefault(first_job[i - 1], []).append((h, i))
                        if h + 1 < H:
                            prep_map.setdefault(first_job[NO - 1], []).append((h + 1, 0))
                            bmap.setdefault(base + (2 * nj) // 3, []).append(h + 1)

                    qh_load(0)
                    for u in range(2 * NKS):
                        emit_unit(0, u, 0)
                        emit_unit(0, u, 1)
                    emit_bounds(0)
                    emit_qT(0, 0)
                    NG = len(gjobs)
                    for g in range(NG):
                        if g in head_start:
                            h = head_start[g]
                            qh_load(h + 1)
                        for h1 in bmap.get(g, []):
                            emit_bounds(h1)
                        for (h1, u, ph2) in fill_map.get(g, []):
                            emit_unit(h1, u, ph2)
                        for (hh, ip) in prep_map.get(g, []):
                            emit_qT(hh, ip)
                        emit_S(gjobs[g])
                        if g >= 3:
                            emit_T(gjobs[g - 3])
                        if g >= 4:
                            emit_AC(gjobs[g - 4])
                    emit_T(gjobs[NG - 3])
                    emit_AC(gjobs[NG - 4])
                    emit_T(gjobs[NG - 2])
                    emit_AC(gjobs[NG - 3])
                    emit_T(gjobs[NG - 1])
                    emit_AC(gjobs[NG - 2])
                    emit_AC(gjobs[NG - 1])
                    flush_o()

                c.run_phase(ph_b)
            c.full_gate()

        for hf in range(2):
            with ExitStack() as stCD:
                ps4 = stCD.enter_context(nc.psum_tensor(f"psCD{hf}", [128, 2048], F32))
                x2 = stCD.enter_context(nc.sbuf_tensor(f"x2_{hf}", [128, DC * TH], F32))

                with ExitStack() as stC:
                    wo_bf = stC.enter_context(nc.sbuf_tensor(f"wo_bf{hf}", [128, (H * VD // 128) * D], BF16))
                    wstC = stC.enter_context(nc.sbuf_tensor(f"wstC{hf}", [128, 2 * 2048], F32))
                    attn_res = stC.enter_context(nc.sbuf_tensor(f"attn_res{hf}", [128, H * TH], BF16))
                    xringC = stC.enter_context(nc.sbuf_tensor(f"xringC{hf}", [128, 2 * D], F32))

                    ICC = H * VD // 128

                    def ph_c(cc, hf=hf):
                        xr_free = [None, None]
                        bank_free = [None, None]
                        ps_free = [None, None]
                        for ii in range(NTH):
                            i = hf * NTH + ii
                            sl = ii % 2
                            dx = "dX0" if sl == 0 else "dX1"
                            cc.waitt("gp", xr_free[sl])
                            cc.ins("gp", lambda e, i=i, sl=sl:
                                   e.dma_start(out=xringC[:, sl * D:(sl + 1) * D], in_=xa[2 * i]),
                                   inc=dx, k=16)
                            xl = cc.tok(dx)
                            cc.waitt("pe", xl)
                            for r in range(DC // 4):
                                pb = (pex, pvb)[r % 2]
                                cc.waitt("pe", bank_free[r % 2])
                                for j in range(4):
                                    dcx = r * 4 + j
                                    cc.ins("pe", lambda e, sl=sl, dcx=dcx, j=j, pb=pb:
                                           e.transpose(pb[:, j * 128:(j + 1) * 128],
                                                       xringC[:, sl * D + dcx * 128: sl * D + (dcx + 1) * 128],
                                                       idn_f[:]),
                                           inc="tP" if j == 3 else None)
                                tr = cc.tok("tP")
                                cc.waitt("dve", tr)
                                for j in range(4):
                                    dcx = r * 4 + j
                                    cc.ins("dve", lambda e, dcx=dcx, ii=ii, j=j, pb=pb:
                                           e.tensor_copy(out=x2[:, dcx * TH + ii * 128: dcx * TH + (ii + 1) * 128],
                                                         in_=pb[:, j * 128:(j + 1) * 128]),
                                           inc="tV" if j == 3 else None)
                                bank_free[r % 2] = cc.tok("tV")
                            xr_free[sl] = cc.tok("tV")
                        x2_ready = cc.tok("tV")

                        cast_tok = [None] * ICC
                        for icx in range(ICC):
                            slot = icx % 2
                            if icx >= 2:
                                cc.waitt("sp", cast_tok[icx - 2])
                            dw = "dW0" if slot == 0 else "dW1"
                            cc.ins("sp", lambda e, icx=icx, slot=slot:
                                   e.dma_start(out=wstC[:, slot * 2048: slot * 2048 + D],
                                               in_=wo_d.ap()[icx * 128:(icx + 1) * 128, :]),
                                   inc=dw, k=16)
                            cc.wait("dve", dw, cc.state["sems"][dw])
                            cast_tok[icx] = cc.ins("dve", lambda e, icx=icx, slot=slot:
                                                   e.tensor_copy(out=wo_bf[:, icx * D:(icx + 1) * D],
                                                                 in_=wstC[:, slot * 2048: slot * 2048 + D]),
                                                   inc="tV")
                        wo_ready = cast_tok[-1]

                        for icx in range(ICC):
                            cc.ins("sp", lambda e, icx=icx, hf=hf:
                                   e.dma_start(out=attn_res[:, icx * TH:(icx + 1) * TH],
                                               in_=attn_a[icx][:, hf * TH:(hf + 1) * TH]),
                                   inc="dS", k=16)
                        attn_tok = cc.tok("dS")

                        nsl = (TH + 511) // 512
                        for oc in range(DC):
                            pbk = oc % 2
                            cc.waitt("pe", ps_free[pbk])
                            cc.waitt("pe", wo_ready)
                            cc.waitt("pe", attn_tok)
                            for icx in range(ICC):
                                for s in range(nsl):
                                    w = min(512, TH - s * 512)
                                    cc.ins("pe", lambda e, pbk=pbk, icx=icx, s=s, w=w, oc=oc,
                                           first=(icx == 0), last=(icx == ICC - 1):
                                           e.matmul(ps4[:, pbk * 1024 + s * 512: pbk * 1024 + s * 512 + w],
                                                    wo_bf[:, icx * D + oc * 128: icx * D + oc * 128 + 128],
                                                    attn_res[:, icx * TH + s * 512: icx * TH + s * 512 + w],
                                                    start=first, stop=last),
                                           inc="tP" if (icx == ICC - 1 and s == nsl - 1) else None)
                            grp = cc.tok("tP")
                            cc.waitt("dve", grp)
                            cc.waitt("dve", x2_ready)
                            cc.ins("dve", lambda e, pbk=pbk, oc=oc:
                                   e.tensor_tensor(out=x2[:, oc * TH:(oc + 1) * TH],
                                                   in0=ps4[:, pbk * 1024: pbk * 1024 + TH],
                                                   in1=x2[:, oc * TH:(oc + 1) * TH], op=ALU.add),
                                   inc="tV")
                            ps_free[pbk] = cc.tok("tV")

                    c.run_phase(ph_c)
                c.full_gate()

                with ExitStack() as stD:
                    xn2 = stD.enter_context(nc.sbuf_tensor(f"xn2_{hf}", [128, DC * TH], BF16))
                    gated = stD.enter_context(nc.sbuf_tensor(f"gated{hf}", [128, SUPC * TH], BF16))
                    wg_bf = stD.enter_context(nc.sbuf_tensor(f"wg_bf{hf}", [128, 2 * DC * SUP], BF16))
                    wu_bf = stD.enter_context(nc.sbuf_tensor(f"wu_bf{hf}", [128, 2 * DC * SUP], BF16))
                    wd_bf = stD.enter_context(nc.sbuf_tensor(f"wd_bf{hf}", [128, SUPC * D], BF16))
                    WSLOT = max(D, 4 * SUP)
                    wtmp_cols = max(2 * WSLOT, 3 * TH)
                    wtmp = stD.enter_context(nc.sbuf_tensor(f"wtmp{hf}", [128, wtmp_cols], F32))
                    silu_t = stD.enter_context(nc.sbuf_tensor(f"silu{hf}", [128, max(2 * 512, TH)], F32))

                    def ph_d(cc, hf=hf):
                        nsl = (TH + 511) // 512
                        rows = wtmp
                        for dcx in range(DC):
                            cc.ins("gp", lambda e, dcx=dcx:
                                   e.tensor_copy(out=xn2[:, dcx * TH:(dcx + 1) * TH],
                                                 in_=x2[:, dcx * TH:(dcx + 1) * TH]),
                                   inc="tG" if dcx == DC - 1 else None)
                        x2b = cc.tok("tG")
                        cc.waitt("pe", x2b)
                        for s in range(nsl):
                            w = min(512, TH - s * 512)
                            for dcx in range(DC):
                                cc.ins("pe", lambda e, s=s, w=w, dcx=dcx,
                                       first=(dcx == 0), last=(dcx == DC - 1):
                                       e.matmul(ps4[0:1, s * 512: s * 512 + w],
                                                ones_neg[:],
                                                xn2[:, dcx * TH + s * 512: dcx * TH + s * 512 + w],
                                                start=first, stop=last),
                                       inc="tP" if dcx == DC - 1 else None)
                        sneg = cc.tok("tP")
                        cc.waitt("gp", sneg)
                        for dcx in range(DC):
                            cc.ins("gp", lambda e, dcx=dcx:
                                   e.tensor_tensor(out=xn2[:, dcx * TH:(dcx + 1) * TH],
                                                   in0=xn2[:, dcx * TH:(dcx + 1) * TH],
                                                   in1=xn2[:, dcx * TH:(dcx + 1) * TH], op=ALU.mult),
                                   inc="tG" if dcx == DC - 1 else None)
                        sq_done = cc.tok("tG")
                        cc.waitt("pe", sq_done)
                        for s in range(nsl):
                            w = min(512, TH - s * 512)
                            for dcx in range(DC):
                                cc.ins("pe", lambda e, s=s, w=w, dcx=dcx,
                                       first=(dcx == 0), last=(dcx == DC - 1):
                                       e.matmul(ps4[0:1, 1024 + s * 512: 1024 + s * 512 + w],
                                                ones_inv[:],
                                                xn2[:, dcx * TH + s * 512: dcx * TH + s * 512 + w],
                                                start=first, stop=last),
                                       inc="tP" if dcx == DC - 1 else None)
                        ssq = cc.tok("tP")
                        cc.waitt("dve", sneg)
                        cc.ins("dve", lambda e: e.tensor_copy(out=rows[0:1, 0:TH], in_=ps4[0:1, 0:TH]), inc="tV")
                        cc.waitt("dve", ssq)
                        cc.ins("dve", lambda e: e.tensor_copy(out=rows[0:1, TH:2 * TH], in_=ps4[0:1, 1024:1024 + TH]), inc="tV")
                        cc.wait("dve", "tV", cc.state["sems"]["tV"])
                        cc.ins("dve", lambda e:
                               e.tensor_tensor(out=rows[0:1, 2 * TH:3 * TH], in0=rows[0:1, 0:TH],
                                               in1=rows[0:1, 0:TH], op=ALU.mult), inc="tV")
                        cc.wait("dve", "tV", cc.state["sems"]["tV"])
                        cc.ins("dve", lambda e:
                               e.tensor_tensor(out=rows[0:1, TH:2 * TH], in0=rows[0:1, TH:2 * TH],
                                               in1=rows[0:1, 2 * TH:3 * TH], op=ALU.subtract), inc="tV")
                        var_tok = cc.tok("tV")
                        cc.waitt("act", var_tok)
                        cc.ins("act", lambda e:
                               e.activation(out=rows[0:1, 2 * TH:3 * TH], in_=rows[0:1, TH:2 * TH],
                                            func=AF.Ln, bias=epsb[0:1, 0:1]), inc="tA")
                        cc.wait("act", "tA", cc.state["sems"]["tA"])
                        cc.ins("act", lambda e:
                               e.activation(out=rows[0:1, 2 * TH:3 * TH], in_=rows[0:1, 2 * TH:3 * TH],
                                            func=AF.Exp, scale=-0.5), inc="tA")
                        rstd_tok = cc.tok("tA")
                        cc.waitt("pe", rstd_tok)
                        for s in range(nsl):
                            w = min(512, TH - s * 512)
                            cc.ins("pe", lambda e, s=s, w=w:
                                   e.matmul(ps4[:, s * 512: s * 512 + w], ones_row[:],
                                            rows[0:1, s * 512: s * 512 + w], start=True, stop=True),
                                   inc="tP")
                            cc.ins("pe", lambda e, s=s, w=w:
                                   e.matmul(ps4[:, 1024 + s * 512: 1024 + s * 512 + w], ones_row[:],
                                            rows[0:1, 2 * TH + s * 512: 2 * TH + s * 512 + w],
                                            start=True, stop=True),
                                   inc="tP")
                        bc_tok = cc.tok("tP")
                        cc.waitt("dve", bc_tok)
                        for dcx in range(DC):
                            cc.ins("dve", lambda e, dcx=dcx:
                                   e.tensor_tensor(out=silu_t[:, 0:TH], in0=x2[:, dcx * TH:(dcx + 1) * TH],
                                                   in1=ps4[:, 0:TH], op=ALU.add), inc="tV")
                            cc.wait("dve", "tV", cc.state["sems"]["tV"])
                            cc.ins("dve", lambda e, dcx=dcx:
                                   e.tensor_tensor(out=xn2[:, dcx * TH:(dcx + 1) * TH],
                                                   in0=silu_t[:, 0:TH], in1=ps4[:, 1024:1024 + TH],
                                                   op=ALU.mult), inc="tV")
                            cc.wait("dve", "tV", cc.state["sems"]["tV"])
                        xn2_tok = cc.tok("tV")

                        st_free = [bc_tok, bc_tok, bc_tok, None]
                        pj = [0]
                        gu_tok = [None, None]
                        wd_tok_box = [None]
                        wgu_free = [None, None]
                        wd_free = [None]

                        def piece(src_ap, dst, doff, war):
                            slot = pj[0] % 4
                            pj[0] += 1
                            dw = ("dW0", "dW1", "dW2", "dW3")[slot]
                            cc.waitt("sp", st_free[slot])
                            cc.ins("sp", lambda e, src_ap=src_ap, slot=slot:
                                   e.dma_start(out=bass.AP(wtmp, slot * 1024,
                                                           [[wtmp_cols, 128], [SUP, 2], [1, SUP]]),
                                               in_=src_ap), inc=dw, k=16)
                            cc.wait("gp", dw, cc.state["sems"][dw])
                            cc.waitt("gp", war)
                            tokc = cc.ins("gp", lambda e, slot=slot, dst=dst, doff=doff:
                                          e.tensor_copy(out=dst[:, doff: doff + 1024],
                                                        in_=wtmp[:, slot * 1024: slot * 1024 + 1024]),
                                          inc="tG")
                            st_free[slot] = tokc
                            return tokc

                        def load_gu(s):
                            sb = s % 2
                            tokc = None
                            for wsrc, wdst in ((wgT_d, wg_bf), (wuT_d, wu_bf)):
                                for g2 in range(DC // 2):
                                    src = bass.AP(wsrc, (2 * g2 * 128) * F + s * SUP,
                                                  [[F, 128], [128 * F, 2], [1, SUP]])
                                    tokc = piece(src, wdst, sb * DC * SUP + 2 * g2 * SUP,
                                                 wgu_free[sb])
                            gu_tok[sb] = tokc

                        def load_wd(s):
                            tokc = None
                            for hcx in range(SUPC):
                                for h2 in range(D // 1024):
                                    src = bass.AP(wdT_d, (s * SUP + hcx * 128) * D + h2 * 1024,
                                                  [[D, 128], [1, 1024]])
                                    tokc = piece(src, wd_bf, hcx * D + h2 * 1024, wd_free[0])
                            wd_tok_box[0] = tokc

                        silu_free = [None, None]
                        gbank = [None, None, None, None]
                        dpsf = [None, None]
                        gated_free = None

                        load_gu(0)
                        load_wd(0)
                        for sp_i in range(NSUP):
                            sb = sp_i % 2
                            wd_tok = wd_tok_box[0]
                            if sp_i + 1 < NSUP:
                                load_gu(sp_i + 1)
                            for hcx in range(SUPC):
                                for s in range(nsl):
                                    w = min(512, TH - s * 512)
                                    gb = s % 2
                                    ub = 2 + s % 2
                                    cc.waitt("pe", gbank[gb])
                                    cc.waitt("pe", gbank[ub])
                                    cc.waitt("pe", gu_tok[sb])
                                    cc.waitt("pe", xn2_tok)
                                    for dcx in range(DC):
                                        cc.ins("pe", lambda e, gb=gb, w=w, s=s, dcx=dcx, hcx=hcx, sb=sb,
                                               first=(dcx == 0), last=(dcx == DC - 1):
                                               e.matmul(ps4[:, gb * 512: gb * 512 + w],
                                                        wg_bf[:, sb * DC * SUP + dcx * SUP + hcx * 128: sb * DC * SUP + dcx * SUP + (hcx + 1) * 128],
                                                        xn2[:, dcx * TH + s * 512: dcx * TH + s * 512 + w],
                                                        start=first, stop=last),
                                               inc="tP" if dcx == DC - 1 else None)
                                    g_grp = cc.tok("tP")
                                    for dcx in range(DC):
                                        cc.ins("pe", lambda e, ub=ub, w=w, s=s, dcx=dcx, hcx=hcx, sb=sb,
                                               first=(dcx == 0), last=(dcx == DC - 1):
                                               e.matmul(ps4[:, ub * 512: ub * 512 + w],
                                                        wu_bf[:, sb * DC * SUP + dcx * SUP + hcx * 128: sb * DC * SUP + dcx * SUP + (hcx + 1) * 128],
                                                        xn2[:, dcx * TH + s * 512: dcx * TH + s * 512 + w],
                                                        start=first, stop=last),
                                               inc="tP" if dcx == DC - 1 else None)
                                    u_grp = cc.tok("tP")
                                    cc.waitt("act", g_grp)
                                    cc.waitt("act", silu_free[s % 2])
                                    cc.ins("act", lambda e, gb=gb, w=w, s=s:
                                           e.activation(out=silu_t[:, (s % 2) * 512:(s % 2) * 512 + w],
                                                        in_=ps4[:, gb * 512: gb * 512 + w],
                                                        func=AF.Silu), inc="tA")
                                    sil = cc.tok("tA")
                                    gbank[gb] = sil
                                    cc.waitt("dve", sil)
                                    cc.waitt("dve", u_grp)
                                    if sp_i > 0 or hcx > 0 or s > 0:
                                        cc.waitt("dve", gated_free)
                                    cc.ins("dve", lambda e, ub=ub, w=w, hcx=hcx, s=s:
                                           e.tensor_tensor(out=gated[:, hcx * TH + s * 512: hcx * TH + s * 512 + w],
                                                           in0=silu_t[:, (s % 2) * 512:(s % 2) * 512 + w],
                                                           in1=ps4[:, ub * 512: ub * 512 + w], op=ALU.mult),
                                           inc="tV")
                                    gbank[ub] = cc.tok("tV")
                                    silu_free[s % 2] = gbank[ub]
                            gated_tok = cc.tok("tV")
                            wgu_free[sb] = cc.tok("tP")
                            cc.waitt("pe", gated_tok)
                            cc.waitt("pe", wd_tok)
                            for dcx in range(DC):
                                for s in range(nsl):
                                    w = min(512, TH - s * 512)
                                    pbi = (dcx * nsl + s) % 2
                                    pb = (pex, pvb)[pbi]
                                    cc.waitt("pe", dpsf[pbi])
                                    for hcx in range(SUPC):
                                        cc.ins("pe", lambda e, pb=pb, w=w, dcx=dcx, s=s, hcx=hcx,
                                               first=(hcx == 0), last=(hcx == SUPC - 1):
                                               e.matmul(pb[:, 0:w],
                                                        wd_bf[:, hcx * D + dcx * 128: hcx * D + (dcx + 1) * 128],
                                                        gated[:, hcx * TH + s * 512: hcx * TH + s * 512 + w],
                                                        start=first, stop=last),
                                               inc="tP" if hcx == SUPC - 1 else None)
                                    dgrp = cc.tok("tP")
                                    cc.waitt("dve", dgrp)
                                    cc.ins("dve", lambda e, pb=pb, w=w, dcx=dcx, s=s:
                                           e.tensor_tensor(out=x2[:, dcx * TH + s * 512: dcx * TH + s * 512 + w],
                                                           in0=pb[:, 0:w],
                                                           in1=x2[:, dcx * TH + s * 512: dcx * TH + s * 512 + w],
                                                           op=ALU.add),
                                           inc="tV")
                                    dpsf[pbi] = cc.tok("tV")
                            gated_free = cc.tok("tP")
                            wd_free[0] = cc.tok("tP")
                            if sp_i + 1 < NSUP:
                                load_wd(sp_i + 1)
                        ffn_done = cc.tok("tV")

                        ost_free = [[st_free[0], st_free[1]], [st_free[2], st_free[3]]]
                        bank_free2 = [None, None]
                        for ii in range(NTH):
                            i = hf * NTH + ii
                            osl = ii % 2
                            cc.waitt("pe", ffn_done)
                            for r in range(DC // 4):
                                pb = (pex, pvb)[r % 2]
                                cc.waitt("pe", bank_free2[r % 2])
                                for j in range(4):
                                    dcx = r * 4 + j
                                    cc.ins("pe", lambda e, dcx=dcx, ii=ii, j=j, pb=pb:
                                           e.transpose(pb[:, j * 128:(j + 1) * 128],
                                                       x2[:, dcx * TH + ii * 128: dcx * TH + (ii + 1) * 128],
                                                       idn_f[:]),
                                           inc="tP" if j == 3 else None)
                                tr = cc.tok("tP")
                                cc.waitt("dve", tr)
                                for _t in ost_free[osl]:
                                    cc.waitt("dve", _t)
                                cc.ins("dve", lambda e, r=r, osl=osl, pb=pb:
                                       e.tensor_copy(out=wtmp[:, osl * WSLOT + r * 512: osl * WSLOT + (r + 1) * 512],
                                                     in_=pb[:, 0:512]),
                                       inc="tV")
                                bank_free2[r % 2] = cc.tok("tV")
                            ev = cc.tok("tV")
                            cc.waitt("sp", ev)
                            dy = "dY0" if osl == 0 else "dY1"
                            cc.ins("sp", lambda e, i=i, osl=osl:
                                   e.dma_start(out=ya[i], in_=wtmp[:, osl * WSLOT: osl * WSLOT + D]),
                                   inc=dy, k=16)
                            ost_free[osl] = [cc.tok(dy)]

                    c.run_phase(ph_d)
                c.full_gate()

        def ph_fin(cc):
            for e in ENGINES:
                for snm in list(cc.state["sems"].keys()):
                    if snm.startswith("d"):
                        cc.wait(e, snm, cc.state["sems"][snm])

        c.run_phase(ph_fin)

    return nc



def rope_tables(S_, dtype=np.float32):
    half = ROPE // 2
    inv = 1.0 / (10000.0 ** (np.arange(half, dtype=np.float64) * 2.0 / ROPE))
    fr = np.arange(S_, dtype=np.float64)[:, None] * inv[None, :]
    cos = np.cos(fr).astype(np.float32)
    sin = np.sin(fr).astype(np.float32)
    cosI = np.repeat(cos, 2, axis=1)
    sinI = np.empty((S_, ROPE), np.float32)
    sinI[:, 0::2] = -sin
    sinI[:, 1::2] = sin
    return cosI, sinI


def host_core_inputs(cfg, xb, weights, p):
    S, D = cfg["S"], cfg["D"]
    NT, NO = S // 128, S // 256
    gl = np.empty(NT, np.int64)
    gl[0::2] = 2 * np.arange(NO) + p
    gl[1::2] = 2 * np.arange(NO) + (1 - p)
    xp = np.ascontiguousarray(xb.reshape(NT, 128, D)[gl].reshape(S, D))

    cosI, sinI = rope_tables(S)
    ck = cosI.reshape(NT, 128, ROPE)[gl]
    sk = sinI.reshape(NT, 128, ROPE)[gl]
    SCALE = float(HD) ** -0.5
    ck_sb = np.ascontiguousarray((ck * SCALE).transpose(1, 0, 2).reshape(128, NT * ROPE))
    sk_sb = np.ascontiguousarray((sk * SCALE).transpose(1, 0, 2).reshape(128, NT * ROPE))
    own = 2 * np.arange(NO) + p
    cq = cosI.reshape(NT, 128, ROPE)[own]
    sq = sinI.reshape(NT, 128, ROPE)[own]
    cq_sb = np.ascontiguousarray(cq.transpose(1, 0, 2).reshape(128, NO * ROPE))
    sq_sb = np.ascontiguousarray(sq.transpose(1, 0, 2).reshape(128, NO * ROPE))

    masks = np.zeros((NO, 128, 512), np.float32)
    for i in range(NO):
        E = 256 * (i + 1)
        w = 512 if E % 512 == 0 else E % 512
        q_glob = 128 * (2 * i + p) + np.arange(128)
        kslot = np.arange(E - w, E)
        ktile = kslot // 128
        k_glob = 128 * gl[ktile] + (kslot % 128)
        m = np.where(k_glob[None, :] > q_glob[:, None], MASK_NEG, 0.0)
        masks[i, :, 512 - w:] = m
        masks[i, :, :512 - w] = 0.0
    masks_sb = np.ascontiguousarray(masks.transpose(1, 0, 2).reshape(128, NO * 512)).astype(ml_dtypes.bfloat16)

    d = dict(weights)
    d["x"] = xp
    d["cosq"] = cq_sb
    d["sinq"] = sq_sb
    d["cosk"] = ck_sb
    d["sink"] = sk_sb
    d["masks"] = masks_sb
    d["idnf"] = np.eye(128, dtype=np.float32)
    d["idnb"] = np.eye(128, dtype=np.float32).astype(ml_dtypes.bfloat16)
    return d, gl, own


_prog_cache = {}


def kernel(L0_attn_hidden_states, L0_attn_Wqa, L0_attn_Wqb, L0_attn_Wkva,
           L0_attn_Wkvb, L0_attn_Wo, L0_ffn_w_gate, L0_ffn_w_up, L0_ffn_w_down):
    cfg = CFG_FULL
    S, D = cfg["S"], cfg["D"]
    B = L0_attn_hidden_states.shape[0]
    NO = S // 256

    key = "full"
    if key not in _prog_cache:
        _prog_cache[key] = build_program(cfg)
    nc = _prog_cache[key]

    weights = {
        "wqa": np.ascontiguousarray(L0_attn_Wqa, np.float32),
        "wqb": np.ascontiguousarray(L0_attn_Wqb, np.float32),
        "wkva": np.ascontiguousarray(L0_attn_Wkva, np.float32),
        "wkvb": np.ascontiguousarray(L0_attn_Wkvb, np.float32),
        "wo": np.ascontiguousarray(L0_attn_Wo, np.float32),
        "wgT": np.ascontiguousarray(np.asarray(L0_ffn_w_gate, np.float32).T),
        "wuT": np.ascontiguousarray(np.asarray(L0_ffn_w_up, np.float32).T),
        "wdT": np.ascontiguousarray(np.asarray(L0_ffn_w_down, np.float32).T),
    }
    x_full = np.asarray(L0_attn_hidden_states, np.float32)

    in_maps = []
    owns = []
    for core in range(8):
        b, p = core // 2, core % 2
        d, gl, own = host_core_inputs(cfg, x_full[b], weights, p)
        in_maps.append(d)
        owns.append(own)

    res = run_bass_kernel_spmd(nc, in_maps, core_ids=list(range(8)))

    out = np.empty((B, S, D), np.float32)
    for core in range(8):
        b, p = core // 2, core % 2
        y = res.results[core]["y"].reshape(NO, 128, D)
        out[b].reshape(S // 128, 128, D)[owns[core]] = y
    return out
